# version 01
# baseline (speedup 1.0000x reference)
import bisect
import numpy as np
from contextlib import ExitStack

import concourse.bass as bass
import concourse.mybir as mybir
from concourse.bass_utils import run_bass_kernel_spmd

F32 = mybir.dt.float32
BF16 = mybir.dt.bfloat16
I32 = mybir.dt.int32
AF = mybir.ActivationFunctionType
ALU = mybir.AluOpType
AXX = mybir.AxisListType.X

DM = 1024
DH = 1024
NST = 16
RNK = 64
DMID = 4096
DQ = 2816
KQ = DQ // 128
KT = DH // 128
MT = DM // 128
KMID = DMID // 128
EPS = 1e-5
QMAX = 127.0
PAIRS = [[0, 1], [2, 3], [4, 5], [6, 7]]
ALL8 = [list(range(8))]


class Tile:
    __slots__ = ("name", "h", "last_w", "readers")

    def __init__(self, name, h):
        self.name = name
        self.h = h
        self.last_w = None
        self.readers = []

    def __getitem__(self, key):
        return self.h[key]


class Prog:
    STREAMS = ("t", "v", "s", "g", "q")
    NQS = 12
    DOMS = (["T", "V", "S", "G", "C"]
            + [f"D{i}" for i in range(12)] + [f"Q{i}" for i in range(12)])

    def __init__(self, nc, es):
        self.nc = nc
        self.stream_ops = {st: [] for st in self.STREAMS}
        self.val = {d: 0 for d in self.DOMS}
        self.sem = {d: es.enter_context(nc.semaphore("sem" + d)) for d in self.DOMS}
        self.hist = {d: ([], []) for d in self.DOMS}
        self.sclock = {st: {d: 0 for d in self.DOMS} for st in self.STREAMS}
        self.dma_idx = {"D": 0, "Q": 0}
        self.n_ops = 0

    def _dom_for(self, st, dma, coll):
        if st == "q":
            q = self.dma_idx["Q"]; self.dma_idx["Q"] += 1
            return f"Q{q % self.NQS}"
        if st == "g":
            if coll:
                return "C"
            if dma:
                q = self.dma_idx["D"]; self.dma_idx["D"] += 1
                return f"D{q % self.NQS}"
            return "G"
        return {"t": "T", "v": "V", "s": "S"}[st]

    def _clock_of(self, d, v):
        vals, clocks = self.hist[d]
        i = bisect.bisect_left(vals, v)
        assert i < len(vals) and vals[i] == v, (d, v)
        return clocks[i]

    def op(self, st, emit, reads=(), writes=(), dma=False, coll=False,
           war_writes=(), track=True):
        dom = self._dom_for(st, dma, coll)
        is_dma = dom[0] in ("D", "Q") and len(dom) > 1
        inc = 16 if is_dma else 1
        clk = self.sclock[st]
        deps = {}

        def need(dep):
            if dep is None:
                return
            dd, vv = dep
            if clk[dd] >= vv:
                return
            if deps.get(dd, 0) < vv:
                deps[dd] = vv

        for tl in reads:
            need(tl.last_w)
        for tl in list(writes) + list(war_writes):
            need(tl.last_w)
            for r in tl.readers:
                need(r)
        if is_dma and self.val[dom] > 0:
            need((dom, self.val[dom]))
        for dd, vv in sorted(deps.items()):
            c2 = self._clock_of(dd, vv)
            for k2, v2 in c2.items():
                if clk[k2] < v2:
                    clk[k2] = v2
            if clk[dd] < vv:
                clk[dd] = vv
        myval = self.val[dom] + inc
        self.val[dom] = myval
        snap = dict(clk)
        snap[dom] = max(snap.get(dom, 0), myval)
        self.hist[dom][0].append(myval)
        self.hist[dom][1].append(snap)

        dep_list = sorted(deps.items())
        sem = self.sem
        semd = sem[dom]

        def emit_fn(eng, _emit=emit, _deps=dep_list, _semd=semd, _inc=inc):
            for dd, vv in _deps[:-1]:
                eng.wait_ge(sem[dd], vv)
            ins = _emit(eng)
            if _deps:
                ins._wait_ge(sem[_deps[-1][0]], _deps[-1][1])
            ins.then_inc(_semd, _inc)
            return ins

        self.stream_ops[st].append(emit_fn)
        self.n_ops += 1
        tok = (dom, myval)
        if track:
            for tl in reads:
                tl.readers.append(tok)
            for tl in writes:
                tl.last_w = tok
                tl.readers = []
        return tok

    def wait(self, st, dep):
        if dep is None:
            return
        dd, vv = dep
        clk = self.sclock[st]
        if clk[dd] >= vv:
            return
        c2 = self._clock_of(dd, vv)
        for k2, v2 in c2.items():
            if clk[k2] < v2:
                clk[k2] = v2
        if clk[dd] < vv:
            clk[dd] = vv
        sem = self.sem

        def emit_fn(eng, _dd=dd, _vv=vv):
            eng.wait_ge(sem[_dd], _vv)
        self.stream_ops[st].append(emit_fn)

    def barrier(self):
        vals = dict(self.val)
        for st in self.STREAMS:
            for d, v in vals.items():
                if v > 0:
                    self.wait(st, (d, v))

    def emit(self):
        so = self.stream_ops
        with self.nc.Block() as block:
            @block.sync
            def _(eng):
                for f in so["q"]:
                    f(eng)

            @block.vector
            def _(eng):
                for f in so["v"]:
                    f(eng)

            @block.scalar
            def _(eng):
                for f in so["s"]:
                    f(eng)

            @block.gpsimd
            def _(eng):
                for f in so["g"]:
                    f(eng)

            @block.tensor
            def _(eng):
                for f in so["t"]:
                    f(eng)


def mm_group(p, psum_tile, items):
    n = len(items)
    tok = None
    for i, (emit, reads) in enumerate(items):
        first = i == 0
        last = i == n - 1
        tok = p.op("t", emit, reads=list(reads),
                   writes=[psum_tile] if last else (),
                   war_writes=[psum_tile] if (first and not last) else ())
    return tok


def build(TKN=8192, TC=512):
    NCH = TKN // TC
    MYT = TKN // 2
    MYC = TC // 2
    TT = TC // 128
    NTT = MYT // 128
    GRP = min(512, MYT)
    NGRP = MYT // GRP
    JPG = GRP // 512 if GRP >= 512 else 1
    JW = min(512, GRP)
    JI = JW // 128

    nc = bass.Bass("TRN2", target_bir_lowering=False)
    es = ExitStack()
    p = Prog(nc, es)

    def din(name, shape, dt=F32):
        return Tile(name, nc.dram_tensor(name, shape, dt, kind="ExternalInput").ap())

    def dscr(name, shape, dt):
        return Tile(name, nc.dram_tensor(name, shape, dt).ap())

    def mksb(stack):
        def sb(name, shape, dt=F32):
            return Tile(name, stack.enter_context(nc.sbuf_tensor(name, shape, dt)))
        return sb

    sbP = mksb(es)

    def pstP(name, shape, dt=F32):
        return Tile(name, es.enter_context(nc.psum_tensor(name, shape, dt)))

    xfull = din("xfull", [TKN, DM])
    xmine = din("xmine", [MYT, DM])
    w_inproj = din("w_inproj", [2 * DH, DM])
    w_conv = din("w_conv", [DH, 4])
    b_conv = din("b_conv", [DH])
    w_xproj = din("w_xproj", [96, DH])
    w_dtproj = din("w_dtproj", [DH, RNK])
    b_dtproj = din("b_dtproj", [DH])
    alog = din("alog", [DH, NST])
    dvec = din("dvec", [DH])
    w_outproj = din("w_outproj", [DM, DH])
    ln1_g = din("ln1_g", [DM]); ln1_b = din("ln1_b", [DM])
    ln2_g = din("ln2_g", [DM]); ln2_b = din("ln2_b", [DM])
    w_ffn1 = din("w_ffn1", [DMID, DM]); b_ffn1 = din("b_ffn1", [DMID])
    w_ffn2 = din("w_ffn2", [DM, DMID]); b_ffn2 = din("b_ffn2", [DM])
    clip_x = din("clip_x", [1]); clip_down = din("clip_down", [1])
    w_gate = din("w_gate", [DQ, DM])
    w_up = din("w_up", [DQ, DM])
    w_down = din("w_down", [DM, DQ])
    out = Tile("out", nc.dram_tensor("out", [MYT, DM], F32, kind="ExternalOutput").ap())

    scr_xdin = dscr("scr_xdin", [NCH, 96, TC], F32)
    scr_xdout = dscr("scr_xdout", [NCH, 96, TC], F32)
    scr_rsin = dscr("scr_rsin", [NCH, TC, DM], F32)
    scr_rsout = dscr("scr_rsout", [NCH, MYC, DM], F32)
    scr_u1 = dscr("scr_u1", [MYT, DM], F32)
    scr_u2f = dscr("scr_u2f", [MT, 128, MYT], BF16)
    scr_xp = dscr("scr_xp", [MYT, DM], F32)
    scr_hpre = dscr("scr_hpre", [KQ, 128, MYT], F32)
    scr_woT = dscr("scr_woT", [MT, 128, KT * 128], BF16)
    scr_w1T = dscr("scr_w1T", [KMID, 128, MT * 128], BF16)
    scr_w2T = dscr("scr_w2T", [MT, 128, KMID * 128], BF16)
    scr_wgT = dscr("scr_wgT", [KQ, 128, MT * 128], BF16)
    scr_wuT = dscr("scr_wuT", [KQ, 128, MT * 128], BF16)
    scr_wdT = dscr("scr_wdT", [MT, 128, KQ * 128], BF16)
    scr_mx = dscr("scr_mx", [128, 1], F32)
    scr_mxo = dscr("scr_mxo", [128, 1], F32)
    scr_mh = dscr("scr_mh", [128, 1], F32)
    scr_mho = dscr("scr_mho", [128, 1], F32)

    wxt = sbP("wxt", [128, KT * 96], BF16)
    wdtp = sbP("wdtp", [64, KT * 128], BF16)
    asbn = sbP("asbn", [128, KT * NST])
    dsb = sbP("dsb", [128, KT])
    cbs = sbP("cbs", [128, KT])
    dtbv = sbP("dtbv", [128, KT])
    wb1 = sbP("wb1", [128, 16])
    cf1 = sbP("cf1", [128, KMID])
    bf2t = sbP("bf2t", [128, MT])
    scg = sbP("scg", [128, KQ])
    scu = sbP("scu", [128, KQ])
    scdt = sbP("scdt", [128, MT])
    idn = sbP("idn", [128, 128], BF16)
    idf = sbP("idf", [128, 128], F32)
    qsc = sbP("qsc", [128, 8])
    lnos = sbP("lnos", [128, 2 * NTT])
    gmaxt = sbP("gmaxt", [128, 4])
    clt = sbP("clt", [128, 2])
    st1 = sbP("st1", [128, 64])
    cns = sbP("cns", [128, 2])
    g2t = sbP("g2t", [128, MT])
    vt1 = sbP("vt1", [128, 512])
    vt2 = sbP("vt2", [128, 512])
    gt1 = sbP("gt1", [128, 512])
    xdbt = sbP("xdbt", [96, 512])
    xdbb = sbP("xdbb", [64, 512], BF16)

    pbA = pstP("pbA", [128, 512])
    pbB = pstP("pbB", [128, 512])
    pbC = pstP("pbC", [128, 512])
    pbD = pstP("pbD", [128, 512])
    pbY = pstP("pbY", [128, 512])
    pbX = pstP("pbX", [96, 512])
    ptB = pstP("ptB", [128, 1024], BF16)
    ptF = pstP("ptF", [128, 512], F32)

    es_w = ExitStack()
    sbW = mksb(es_w)
    wit = sbW("wit", [128, MT * 2048], BF16)
    wcv = sbW("wcv", [128, KT * 4 * 128], BF16)

    es_pp = ExitStack()
    sbp = mksb(es_pp)
    io1 = sbp("io1", [128, 128])
    io2 = sbp("io2", [128, 128])
    p.op("v", lambda v: v.memset(cns[:, 0:1], EPS), writes=[cns])
    p.op("v", lambda v: v.memset(cns[:, 1:2], -0.5), writes=[cns])
    p.op("g", lambda g: g.iota(io1[:, :], [[1, 128]], channel_multiplier=0,
                               allow_small_or_imprecise_dtypes=True), writes=[io1])
    p.op("g", lambda g: g.iota(io2[:, :], [[0, 128]], channel_multiplier=1,
                               allow_small_or_imprecise_dtypes=True), writes=[io2])
    p.op("v", lambda v: v.tensor_tensor(idn[:, :], io1[:, :], io2[:, :], ALU.is_equal),
         reads=[io1, io2], writes=[idn])
    p.op("v", lambda v: v.tensor_tensor(idf[:, :], io1[:, :], io2[:, :], ALU.is_equal),
         reads=[io1, io2], writes=[idf])

    def load_vec(dst, src, n, col0=0):
        kk = n // 128
        p.op("g", lambda g, _c=col0, _k=kk: g.dma_start(
            out=dst[:, _c:_c + _k],
            in_=bass.AP(src.h.tensor, src.h.offset, [[1, 128], [128, _k]]),
            allow_slow_non_contiguous=True), reads=[src], writes=[dst], dma=True)

    load_vec(dsb, dvec, DH)
    load_vec(cbs, b_conv, DH)
    load_vec(dtbv, b_dtproj, DH)
    g1t = sbp("g1t", [128, MT]); b1tt = sbp("b1tt", [128, MT], BF16)
    b2tt = sbp("b2tt", [128, MT], BF16)
    bf1t = sbp("bf1t", [128, KMID])
    load_vec(g1t, ln1_g, DM)
    load_vec(b1tt, ln1_b, DM)
    load_vec(g2t, ln2_g, DM)
    load_vec(b2tt, ln2_b, DM)
    load_vec(bf2t, b_ffn2, DM)
    load_vec(bf1t, b_ffn1, DMID)

    atmp = sbp("atmp", [128, KT * NST])
    for k in range(KT):
        p.op("g", lambda g, k=k: g.dma_start(
            out=atmp[:, k * NST:(k + 1) * NST], in_=alog[k * 128:(k + 1) * 128, :]),
            reads=[alog], writes=[atmp], dma=True)
    p.op("s", lambda s: s.activation(asbn[:, :], atmp[:, :], AF.Exp),
         reads=[atmp], writes=[asbn])
    p.op("v", lambda v: v.tensor_scalar(asbn[:, :], asbn[:, :], -1.0, None, ALU.mult),
         reads=[asbn], writes=[asbn])

    cvt = sbp("cvt", [128, KT * 4])
    for k in range(KT):
        p.op("g", lambda g, k=k: g.dma_start(
            out=cvt[:, k * 4:(k + 1) * 4], in_=w_conv[k * 128:(k + 1) * 128, :]),
            reads=[w_conv], writes=[cvt], dma=True)
    for k in range(KT):
        for tap in range(4):
            col = (k * 4 + tap) * 128
            p.op("v", lambda v, k=k, tap=tap, col=col: v.scalar_tensor_tensor(
                wcv[:, col:col + 128], idn[:, :], 1.0,
                cvt[:, k * 4 + tap:k * 4 + tap + 1].broadcast_to([128, 128]),
                ALU.mult, ALU.mult),
                reads=[idn, cvt], writes=[wcv])

    wstage = sbp("wstage", [128, 8192], BF16)
    wloadf = sbp("wloadf", [128, DQ])
    wqi = sbp("wqi", [128, DM], I32)
    wq1 = sbp("wq1", [128, DM])
    wq2 = sbp("wq2", [128, DM])
    wq3 = sbp("wq3", [128, DM], BF16)

    for m in range(16):
        p.op("g", lambda g, m=m: g.dma_start(
            out=wstage[:, 0:DM], in_=w_inproj[m * 128:(m + 1) * 128, :]),
            reads=[w_inproj], writes=[wstage], dma=True)
        for k in range(MT):
            p.op("t", lambda t, k=k: t.transpose(
                ptB[:, 0:128], wstage[:, k * 128:(k + 1) * 128], idn[:, :]),
                reads=[wstage, idn], writes=[ptB])
            p.op("s", lambda s, m=m, k=k: s.copy(
                wit[:, k * 2048 + m * 128:k * 2048 + m * 128 + 128],
                ptB[:, 0:128]), reads=[ptB], writes=[wit])

    for m in range(16):
        items = []
        for k in range(MT):
            def em(t, m=m, k=k, first=(k == 0), last=(k == MT - 1)):
                return t.matmul(pbA[:, 0:1],
                                wit[:, k * 2048 + m * 128:k * 2048 + m * 128 + 128],
                                b1tt[:, k:k + 1], start=first, stop=last)
            items.append((em, [wit, b1tt]))
        mm_group(p, pbA, items)
        p.op("s", lambda s, m=m: s.copy(wb1[:, m:m + 1], pbA[:, 0:1]),
             reads=[pbA], writes=[wb1])
    for k in range(MT):
        p.op("v", lambda v, k=k: v.tensor_scalar(
            wit[:, k * 2048:(k + 1) * 2048], wit[:, k * 2048:(k + 1) * 2048],
            g1t[:, k:k + 1], None, ALU.mult),
            reads=[wit, g1t], writes=[wit])

    for k in range(KT):
        p.op("g", lambda g, k=k: g.dma_start(
            out=wstage[0:96, 0:128], in_=w_xproj[:, k * 128:(k + 1) * 128],
            allow_slow_non_contiguous=True),
            reads=[w_xproj], writes=[wstage], dma=True)
        p.op("t", lambda t: t.transpose(ptB[:, 0:128], wstage[:, 0:128], idn[:, :]),
             reads=[wstage, idn], writes=[ptB])
        p.op("s", lambda s, k=k: s.copy(wxt[:, k * 96:(k + 1) * 96], ptB[:, 0:96]),
             reads=[ptB], writes=[wxt])
    for m in range(KT):
        p.op("g", lambda g, m=m: g.dma_start(
            out=wstage[:, 0:RNK], in_=w_dtproj[m * 128:(m + 1) * 128, :]),
            reads=[w_dtproj], writes=[wstage], dma=True)
        p.op("t", lambda t: t.transpose(ptB[:, 0:128], wstage[:, 0:128], idn[:, :]),
             reads=[wstage, idn], writes=[ptB])
        p.op("s", lambda s, m=m: s.copy(wdtp[0:64, m * 128:(m + 1) * 128], ptB[0:64, 0:128]),
             reads=[ptB], writes=[wdtp])

    def prep_dram_T(src, rows, cols, dst_scr, fold_dst=None, fold_beta=None):
        nk = cols // 128
        for m in range(rows // 128):
            p.op("g", lambda g, m=m: g.dma_start(
                out=wstage[:, 0:cols], in_=src[m * 128:(m + 1) * 128, :]),
                reads=[src], writes=[wstage], dma=True)
            for k in range(nk):
                p.op("t", lambda t, k=k: t.transpose(
                    ptB[:, 0:128], wstage[:, k * 128:(k + 1) * 128], idn[:, :]),
                    reads=[wstage, idn], writes=[ptB])
                p.op("s", lambda s, k=k: s.copy(
                    wstage[:, 4096 + k * 128:4096 + k * 128 + 128], ptB[:, 0:128]),
                    reads=[ptB], writes=[wstage])
            p.op("q", lambda q, m=m, nk=nk: q.dma_start(
                out=dst_scr.h[m], in_=wstage[:, 4096:4096 + nk * 128]),
                reads=[wstage], writes=[dst_scr], dma=True)
            if fold_dst is not None:
                items = []
                for k in range(nk):
                    def em(t, k=k, first=(k == 0), last=(k == nk - 1)):
                        return t.matmul(pbB[:, 0:1],
                                        wstage[:, 4096 + k * 128:4096 + k * 128 + 128],
                                        fold_beta[:, k:k + 1], start=first, stop=last)
                    items.append((em, [wstage, fold_beta]))
                mm_group(p, pbB, items)
                p.op("s", lambda s, m=m: s.copy(fold_dst[:, m:m + 1], pbB[:, 0:1]),
                     reads=[pbB], writes=[fold_dst])

    prep_dram_T(w_outproj, DM, DH, scr_woT)
    prep_dram_T(w_ffn1, DMID, DM, scr_w1T, fold_dst=cf1, fold_beta=b2tt)
    p.op("v", lambda v: v.tensor_tensor(cf1[:, :], cf1[:, :], bf1t[:, :], ALU.add),
         reads=[cf1, bf1t], writes=[cf1])
    prep_dram_T(w_ffn2, DM, DMID, scr_w2T)

    def prep_quant(src, rows, cols, dst_scr, sc_tile):
        nk = cols // 128
        for m in range(rows // 128):
            p.op("g", lambda g, m=m: g.dma_start(
                out=wloadf[:, 0:cols], in_=src[m * 128:(m + 1) * 128, :]),
                reads=[src], writes=[wloadf], dma=True)
            p.op("v", lambda v, cols=cols: v.tensor_reduce(
                st1[:, 60:61], wloadf[:, 0:cols], AXX, ALU.max,
                apply_absolute_value=True), reads=[wloadf], writes=[st1])
            p.op("v", lambda v, m=m: v.tensor_scalar(
                sc_tile[:, m:m + 1], st1[:, 60:61], 1.0 / QMAX, None, ALU.mult),
                reads=[st1], writes=[sc_tile])
            p.op("v", lambda v: v.reciprocal(st1[:, 61:62], st1[:, 60:61]),
                 reads=[st1], writes=[st1])
            p.op("v", lambda v: v.tensor_scalar(
                st1[:, 61:62], st1[:, 61:62], QMAX, None, ALU.mult),
                reads=[st1], writes=[st1])
            for j0 in range(0, cols, DM):
                w = min(DM, cols - j0)
                p.op("v", lambda v, j0=j0, w=w: v.tensor_scalar(
                    wq1[:, 0:w], wloadf[:, j0:j0 + w], st1[:, 61:62], None, ALU.mult),
                    reads=[wloadf, st1], writes=[wq1])
                p.op("v", lambda v, w=w: v.tensor_copy(wqi[:, 0:w], wq1[:, 0:w]),
                     reads=[wq1], writes=[wqi])
                p.op("v", lambda v, w=w: v.tensor_copy(wq2[:, 0:w], wqi[:, 0:w]),
                     reads=[wqi], writes=[wq2])
                p.op("v", lambda v, w=w: v.tensor_tensor(
                    wq1[:, 0:w], wq1[:, 0:w], wq2[:, 0:w], ALU.subtract),
                    reads=[wq1, wq2], writes=[wq1])
                p.op("v", lambda v, w=w: v.scalar_tensor_tensor(
                    wq2[:, 0:w], wq1[:, 0:w], 0.5, wq2[:, 0:w], ALU.is_gt, ALU.add),
                    reads=[wq1, wq2], writes=[wq2])
                p.op("v", lambda v, w=w: v.scalar_tensor_tensor(
                    wq2[:, 0:w], wq1[:, 0:w], -0.5, wq2[:, 0:w], ALU.is_lt, ALU.subtract),
                    reads=[wq1, wq2], writes=[wq2])
                p.op("v", lambda v, w=w: v.tensor_scalar(
                    wq3[:, 0:w], wq2[:, 0:w], -QMAX - 1.0, QMAX, ALU.max, ALU.min),
                    reads=[wq2], writes=[wq3])
                for kk in range(w // 128):
                    k = (j0 // 128) + kk
                    p.op("t", lambda t, kk=kk: t.transpose(
                        ptB[:, 0:128], wq3[:, kk * 128:(kk + 1) * 128], idn[:, :]),
                        reads=[wq3, idn], writes=[ptB])
                    p.op("s", lambda s, k=k: s.copy(
                        wstage[:, 4096 + k * 128:4096 + k * 128 + 128], ptB[:, 0:128]),
                        reads=[ptB], writes=[wstage])
            p.op("q", lambda q, m=m, nk=nk: q.dma_start(
                out=dst_scr.h[m], in_=wstage[:, 4096:4096 + nk * 128]),
                reads=[wstage], writes=[dst_scr], dma=True)

    prep_quant(w_gate, DQ, DM, scr_wgT, scg)
    prep_quant(w_up, DQ, DM, scr_wuT, scu)
    prep_quant(w_down, DM, DQ, scr_wdT, scdt)

    p.barrier()
    es_pp.close()

    def ln_stats(src_tile, ntt, scol, junk_bf, junk_f32):
        for i in range(ntt):
            p.op("v", lambda v, i=i: v.tensor_scalar(
                junk_bf[:, 0:DM], src_tile[:, i * DM:(i + 1) * DM],
                1.0, 0.0, ALU.mult, ALU.add, accum_out=st1[:, scol + i:scol + i + 1]),
                reads=[src_tile], writes=[junk_bf, st1])
            p.op("s", lambda s, i=i: s.activation(
                junk_f32[:, 0:DM], src_tile[:, i * DM:(i + 1) * DM],
                AF.Square, accum_out=st1[:, scol + 4 + i:scol + 4 + i + 1]),
                reads=[src_tile], writes=[junk_f32, st1])
        p.op("v", lambda v: v.tensor_scalar(
            st1[:, scol:scol + ntt], st1[:, scol:scol + ntt], 1.0 / DM, None, ALU.mult),
            reads=[st1], writes=[st1])
        p.op("v", lambda v: v.tensor_scalar(
            st1[:, scol + 4:scol + 4 + ntt], st1[:, scol + 4:scol + 4 + ntt],
            1.0 / DM, None, ALU.mult), reads=[st1], writes=[st1])
        p.op("v", lambda v: v.scalar_tensor_tensor(
            st1[:, scol + 12:scol + 12 + ntt], st1[:, scol:scol + ntt], 1.0,
            st1[:, scol:scol + ntt], ALU.mult, ALU.mult),
            reads=[st1], writes=[st1])
        p.op("v", lambda v: v.tensor_tensor(
            st1[:, scol + 4:scol + 4 + ntt], st1[:, scol + 4:scol + 4 + ntt],
            st1[:, scol + 12:scol + 12 + ntt], ALU.subtract),
            reads=[st1], writes=[st1])
        p.op("s", lambda s: s.activation(
            st1[:, scol + 12:scol + 12 + ntt], st1[:, scol + 4:scol + 4 + ntt],
            AF.Ln, bias=cns[:, 0:1]), reads=[st1, cns], writes=[st1])
        p.op("s", lambda s: s.activation(
            st1[:, scol + 8:scol + 8 + ntt], st1[:, scol + 12:scol + 12 + ntt],
            AF.Exp, scale=cns[:, 1:2]), reads=[st1, cns], writes=[st1])
        p.op("v", lambda v: v.scalar_tensor_tensor(
            st1[:, scol + 16:scol + 16 + ntt], st1[:, scol:scol + ntt], -1.0,
            st1[:, scol + 8:scol + 8 + ntt], ALU.mult, ALU.mult),
            reads=[st1], writes=[st1])

    es_p1 = ExitStack()
    sb1 = mksb(es_p1)
    xtk = sb1("xtk", [128, TT * DM])
    u0t = sb1("u0t", [128, TT * DM])
    ubf = sb1("ubf", [128, TT * DM], BF16)
    ufe = sb1("ufe", [128, MT * TC], BF16)
    xib = sb1("xib", [128, KT * (TC + 3)], BF16)
    sztf = sb1("sztf", [128, KT * TC])
    xcf = sb1("xcf", [128, KT * TC])
    xcb = sb1("xcb", [128, KT * TC], BF16)
    dtf = sb1("dtf", [128, KT * TC])
    brep = sb1("brep", [128, NST * TC])
    crep = sb1("crep", [128, NST * TC])
    dxc = sb1("dxc", [128, 2 * TC])
    dab = sb1("dab", [128, 4 * TC])
    dbxt = sb1("dbxt", [128, 4 * TC])
    hall = sb1("hall", [128, 4 * TC])
    qnb = sb1("qnb", [128, 4 * TC], BF16)
    hcar = sb1("hcar", [128, KT * NST])
    ygb = sb1("ygb", [128, KT * TC], BF16)
    ofes = sb1("ofes", [128, TC])
    wots = sb1("wots", [128, 2 * KT * 128], BF16)

    p.op("v", lambda v: v.memset(hcar[:, :], 0.0), writes=[hcar])
    p.op("v", lambda v: v.memset(xib[:, :], 0.0), writes=[xib])

    for c in range(NCH):
        for i in range(TT):
            p.op("q", lambda q, c=c, i=i: q.dma_start(
                out=xtk[:, i * DM:(i + 1) * DM],
                in_=xfull[c * TC + i * 128: c * TC + (i + 1) * 128, :]),
                reads=[xfull], writes=[xtk], dma=True)
        ln_stats(xtk, TT, 0, ubf, u0t)
        for i in range(TT):
            p.op("s", lambda s, i=i: s.activation(
                u0t[:, i * DM:(i + 1) * DM], xtk[:, i * DM:(i + 1) * DM], AF.Identity,
                scale=st1[:, 8 + i:9 + i], bias=st1[:, 16 + i:17 + i]),
                reads=[xtk, st1], writes=[u0t])
        ln_stats(u0t, TT, 0, ubf, xtk)
        for i in range(TT):
            p.op("s", lambda s, i=i: s.activation(
                ubf[:, i * DM:(i + 1) * DM], u0t[:, i * DM:(i + 1) * DM], AF.Identity,
                scale=st1[:, 8 + i:9 + i], bias=st1[:, 16 + i:17 + i]),
                reads=[u0t, st1], writes=[ubf])
        for i in range(TT):
            for k in range(MT):
                p.op("t", lambda t, i=i, k=k: t.transpose(
                    ptB[:, 0:128], ubf[:, i * DM + k * 128:i * DM + k * 128 + 128],
                    idn[:, :]), reads=[ubf, idn], writes=[ptB])
                p.op("s", lambda s, i=i, k=k: s.copy(
                    ufe[:, k * TC + i * 128:k * TC + i * 128 + 128], ptB[:, 0:128]),
                    reads=[ptB], writes=[ufe])
        for m in range(16):
            pb = pbA if m % 2 == 0 else pbB
            items = []
            for k in range(MT):
                def em(t, m=m, k=k, pb=pb, first=(k == 0), last=(k == MT - 1)):
                    return t.matmul(pb[:, 0:TC],
                                    wit[:, k * 2048 + m * 128:k * 2048 + m * 128 + 128],
                                    ufe[:, k * TC:(k + 1) * TC], start=first, stop=last)
                items.append((em, [wit, ufe]))
            mm_group(p, pb, items)
            if m < 8:
                p.op("s", lambda s, m=m: s.copy(
                    xib[:, m * (TC + 3):m * (TC + 3) + 3],
                    xib[:, m * (TC + 3) + TC:m * (TC + 3) + TC + 3]),
                    reads=[xib], writes=[xib])
                p.op("s", lambda s, m=m, pb=pb: s.activation(
                    xib[:, m * (TC + 3) + 3:m * (TC + 3) + 3 + TC], pb[:, 0:TC],
                    AF.Identity, bias=wb1[:, m:m + 1]),
                    reads=[pb, wb1], writes=[xib])
            else:
                kk = m - 8
                p.op("s", lambda s, kk=kk, m=m, pb=pb: s.activation(
                    sztf[:, kk * TC:(kk + 1) * TC], pb[:, 0:TC],
                    AF.Silu, bias=wb1[:, m:m + 1]),
                    reads=[pb, wb1], writes=[sztf])
        for k in range(KT):
            pb = pbC if k % 2 == 0 else pbD
            items = []
            for tap in range(4):
                def em(t, k=k, tap=tap, pb=pb, first=(tap == 0), last=(tap == 3)):
                    return t.matmul(pb[:, 0:TC],
                                    wcv[:, (k * 4 + tap) * 128:(k * 4 + tap) * 128 + 128],
                                    xib[:, k * (TC + 3) + tap:k * (TC + 3) + tap + TC],
                                    start=first, stop=last)
                items.append((em, [wcv, xib]))
            mm_group(p, pb, items)
            p.op("s", lambda s, k=k, pb=pb: s.activation(
                xcf[:, k * TC:(k + 1) * TC], pb[:, 0:TC], AF.Silu, bias=cbs[:, k:k + 1]),
                reads=[pb, cbs], writes=[xcf])
            p.op("v", lambda v, k=k: v.tensor_copy(
                xcb[:, k * TC:(k + 1) * TC], xcf[:, k * TC:(k + 1) * TC]),
                reads=[xcf], writes=[xcb])
        items = []
        for k in range(KT):
            def em(t, k=k, first=(k == 0), last=(k == KT - 1)):
                return t.matmul(pbX[:, 0:TC], wxt[:, k * 96:(k + 1) * 96],
                                xcb[:, k * TC:(k + 1) * TC], start=first, stop=last)
            items.append((em, [wxt, xcb]))
        mm_group(p, pbX, items)
        p.op("s", lambda s: s.copy(vt1[0:96, 0:TC], pbX[0:96, 0:TC]),
             reads=[pbX], writes=[vt1])
        p.op("q", lambda q, c=c: q.dma_start(
            out=scr_xdin.h[c], in_=vt1[0:96, 0:TC]),
            reads=[vt1], writes=[scr_xdin], dma=True)
        p.op("g", lambda g, c=c: g.collective_compute(
            "AllReduce", ALU.add, replica_groups=PAIRS,
            ins=[scr_xdin.h[c].opt()], outs=[scr_xdout.h[c].opt()]),
            reads=[scr_xdin], writes=[scr_xdout], coll=True)
        p.op("q", lambda q, c=c: q.dma_start(
            out=xdbt[:, 0:TC], in_=scr_xdout.h[c]),
            reads=[scr_xdout], writes=[xdbt], dma=True)
        p.op("v", lambda v: v.tensor_copy(xdbb[0:64, 0:TC], xdbt[0:64, 0:TC]),
             reads=[xdbt], writes=[xdbb])
        for m in range(KT):
            pb = pbC if m % 2 == 0 else pbD
            p.op("t", lambda t, m=m, pb=pb: t.matmul(
                pb[:, 0:TC], wdtp[0:64, m * 128:(m + 1) * 128], xdbb[0:64, 0:TC],
                start=True, stop=True), reads=[wdtp, xdbb], writes=[pb])
            p.op("s", lambda s, m=m, pb=pb: s.activation(
                vt1[:, 0:TC], pb[:, 0:TC], AF.Identity, bias=dtbv[:, m:m + 1]),
                reads=[pb, dtbv], writes=[vt1])
            p.op("s", lambda s, m=m, pb=pb: s.activation(
                vt2[:, 0:TC], pb[:, 0:TC], AF.Abs, bias=dtbv[:, m:m + 1]),
                reads=[pb, dtbv], writes=[vt2])
            p.op("s", lambda s: s.activation(vt2[:, 0:TC], vt2[:, 0:TC],
                                             AF.Exp, scale=-1.0),
                 reads=[vt2], writes=[vt2])
            p.op("s", lambda s: s.activation(vt2[:, 0:TC], vt2[:, 0:TC],
                                             AF.Ln, bias=1.0),
                 reads=[vt2], writes=[vt2])
            p.op("v", lambda v, m=m: v.scalar_tensor_tensor(
                dtf[:, m * TC:(m + 1) * TC], vt1[:, 0:TC], 0.0, vt2[:, 0:TC],
                ALU.max, ALU.add), reads=[vt1, vt2], writes=[dtf])
        p.op("q", lambda q: q.dma_start(
            out=brep[0:1, :], in_=xdbt[64:80, 0:TC],
            allow_slow_non_contiguous=True),
            reads=[xdbt], writes=[brep], dma=True)
        p.op("q", lambda q: q.dma_start(
            out=crep[0:1, :], in_=xdbt[80:96, 0:TC],
            allow_slow_non_contiguous=True),
            reads=[xdbt], writes=[crep], dma=True)
        nrep = 1
        while nrep < 128:
            p.op("q", lambda q, n=nrep: q.dma_start(
                out=brep[n:2 * n, :], in_=brep[0:n, :]),
                reads=[brep], writes=[brep], dma=True)
            p.op("q", lambda q, n=nrep: q.dma_start(
                out=crep[n:2 * n, :], in_=crep[0:n, :]),
                reads=[crep], writes=[crep], dma=True)
            nrep *= 2
        for k in range(KT):
            p.op("v", lambda v, k=k: v.tensor_tensor(
                dxc[:, (k % 2) * TC:(k % 2 + 1) * TC], dtf[:, k * TC:(k + 1) * TC],
                xcf[:, k * TC:(k + 1) * TC], ALU.mult),
                reads=[dtf, xcf], writes=[dxc])
            yitems = []
            for n in range(NST):
                sl = n % 4
                p.op("s", lambda s, k=k, n=n, sl=sl: s.activation(
                    dab[:, sl * TC:(sl + 1) * TC], dtf[:, k * TC:(k + 1) * TC],
                    AF.Exp, scale=asbn[:, k * NST + n:k * NST + n + 1]),
                    reads=[dtf, asbn], writes=[dab])
                p.op("g", lambda g, k=k, n=n, sl=sl: g.tensor_tensor(
                    dbxt[:, sl * TC:(sl + 1) * TC], dxc[:, (k % 2) * TC:(k % 2 + 1) * TC],
                    brep[:, n * TC:(n + 1) * TC], ALU.mult),
                    reads=[dxc, brep], writes=[dbxt])
                if c == 0:
                    ini = 0.0
                    rds = [dab, dbxt]
                else:
                    ini = hcar[:, k * NST + n:k * NST + n + 1]
                    rds = [dab, dbxt, hcar]
                p.op("v", lambda v, sl=sl, ini=ini: v.tensor_tensor_scan(
                    hall[:, sl * TC:(sl + 1) * TC], dab[:, sl * TC:(sl + 1) * TC],
                    dbxt[:, sl * TC:(sl + 1) * TC], ini, ALU.mult, ALU.add),
                    reads=rds, writes=[hall])
                p.op("v", lambda v, n=n, sl=sl: v.tensor_tensor(
                    qnb[:, sl * TC:(sl + 1) * TC], hall[:, sl * TC:(sl + 1) * TC],
                    crep[:, n * TC:(n + 1) * TC], ALU.mult),
                    reads=[hall, crep], writes=[qnb])
                def em(t, n=n, sl=sl, first=(n == 0), last=(n == NST - 1)):
                    return t.matmul(pbY[:, 0:TC], idn[:, :],
                                    qnb[:, sl * TC:(sl + 1) * TC],
                                    start=first, stop=last)
                yitems.append((em, [idn, qnb]))
                if sl == 3:
                    p.op("v", lambda v, k=k, n=n: v.tensor_copy(
                        hcar[:, k * NST + n - 3:k * NST + n + 1],
                        bass.AP(hall.h, TC - 1, [[4 * TC, 128], [TC, 4]])),
                        reads=[hall], writes=[hcar])
            mm_group(p, pbY, yitems)
            p.op("v", lambda v, k=k: v.scalar_tensor_tensor(
                gt1[:, 0:TC], xcf[:, k * TC:(k + 1) * TC], dsb[:, k:k + 1],
                pbY[:, 0:TC], ALU.mult, ALU.add), reads=[xcf, dsb, pbY], writes=[gt1])
            p.op("v", lambda v, k=k: v.tensor_tensor(
                ygb[:, k * TC:(k + 1) * TC], gt1[:, 0:TC],
                sztf[:, k * TC:(k + 1) * TC],
                ALU.mult), reads=[gt1, sztf], writes=[ygb])
        for m in range(MT):
            p.op("q", lambda q, m=m: q.dma_start(
                out=wots[:, (m % 2) * 1024:(m % 2) * 1024 + 1024],
                in_=scr_woT.h[m]), reads=[scr_woT], writes=[wots], dma=True)
            pb = pbA if m % 2 == 0 else pbB
            items = []
            for k in range(KT):
                def em(t, m=m, k=k, pb=pb, first=(k == 0), last=(k == KT - 1)):
                    return t.matmul(pb[:, 0:TC],
                                    wots[:, (m % 2) * 1024 + k * 128:(m % 2) * 1024 + k * 128 + 128],
                                    ygb[:, k * TC:(k + 1) * TC], start=first, stop=last)
                items.append((em, [wots, ygb]))
            mm_group(p, pb, items)
            p.op("s", lambda s, m=m, pb=pb: s.copy(
                ofes[:, 0:TC], pb[:, 0:TC]), reads=[pb], writes=[ofes])
            for i in range(TT):
                p.op("t", lambda t, m=m, i=i: t.transpose(
                    ptF[:, 0:128], ofes[:, i * 128:i * 128 + 128],
                    idf[:, :]), reads=[ofes, idf], writes=[ptF])
                p.op("s", lambda s, m=m, i=i: s.copy(
                    xtk[:, i * DM + m * 128:i * DM + m * 128 + 128], ptF[:, 0:128]),
                    reads=[ptF], writes=[xtk])
        for i in range(TT):
            p.op("q", lambda q, c=c, i=i: q.dma_start(
                out=scr_rsin.h[c, i * 128:(i + 1) * 128, :],
                in_=xtk[:, i * DM:(i + 1) * DM]),
                reads=[xtk], writes=[scr_rsin], dma=True)
        p.op("g", lambda g, c=c: g.collective_compute(
            "ReduceScatter", ALU.add, replica_groups=PAIRS,
            ins=[scr_rsin.h[c].opt()], outs=[scr_rsout.h[c].opt()]),
            reads=[scr_rsin], writes=[scr_rsout], coll=True)

    ntt2 = MYC // 128
    for c in range(NCH):
        for i in range(ntt2):
            p.op("q", lambda q, c=c, i=i: q.dma_start(
                out=xtk[:, i * DM:(i + 1) * DM],
                in_=xmine[c * MYC + i * 128:c * MYC + (i + 1) * 128, :]),
                reads=[xmine], writes=[xtk], dma=True)
        ln_stats(xtk, ntt2, 0, ubf, u0t)
        for i in range(ntt2):
            p.op("s", lambda s, i=i: s.activation(
                u0t[:, i * DM:(i + 1) * DM], xtk[:, i * DM:(i + 1) * DM], AF.Identity,
                scale=st1[:, 8 + i:9 + i], bias=st1[:, 16 + i:17 + i]),
                reads=[xtk, st1], writes=[u0t])
        for i in range(ntt2):
            p.op("q", lambda q, c=c, i=i, o=ntt2 + i: q.dma_start(
                out=xtk[:, o * DM:(o + 1) * DM],
                in_=scr_rsout.h[c, i * 128:(i + 1) * 128, :]),
                reads=[scr_rsout], writes=[xtk], dma=True)
        for i in range(ntt2):
            p.op("v", lambda v, i=i, o=ntt2 + i: v.tensor_tensor(
                u0t[:, i * DM:(i + 1) * DM], u0t[:, i * DM:(i + 1) * DM],
                xtk[:, o * DM:(o + 1) * DM], ALU.add),
                reads=[u0t, xtk], writes=[u0t])
        for i in range(ntt2):
            p.op("q", lambda q, c=c, i=i: q.dma_start(
                out=scr_u1.h[c * MYC + i * 128:c * MYC + (i + 1) * 128, :],
                in_=u0t[:, i * DM:(i + 1) * DM]), reads=[u0t], writes=[scr_u1], dma=True)
        ln_stats(u0t, ntt2, 0, ubf, xtk)
        for i in range(ntt2):
            p.op("s", lambda s, i=i: s.activation(
                ubf[:, i * DM:(i + 1) * DM], u0t[:, i * DM:(i + 1) * DM], AF.Identity,
                scale=st1[:, 8 + i:9 + i], bias=st1[:, 16 + i:17 + i]),
                reads=[u0t, st1], writes=[ubf])
        for i in range(ntt2):
            for k in range(MT):
                p.op("t", lambda t, i=i, k=k: t.transpose(
                    ptB[:, 0:128], ubf[:, i * DM + k * 128:i * DM + k * 128 + 128],
                    idn[:, :]), reads=[ubf, idn], writes=[ptB])
                p.op("s", lambda s, k=k: s.activation(
                    qnb[:, 0:128], ptB[:, 0:128], AF.Identity, scale=g2t[:, k:k + 1]),
                    reads=[ptB, g2t], writes=[qnb])
                p.op("q", lambda q, c=c, i=i, k=k: q.dma_start(
                    out=scr_u2f.h[k, :, c * MYC + i * 128:c * MYC + (i + 1) * 128],
                    in_=qnb[:, 0:128]), reads=[qnb], writes=[scr_u2f], dma=True)

    p.barrier()
    es_p1.close()
    es_w.close()

    es_p3 = ExitStack()
    sb3 = mksb(es_p3)
    fgrp = sb3("fgrp", [128, KMID * GRP], BF16)
    u2g = sb3("u2g", [128, MT * GRP], BF16)
    w1s = sb3("w1s", [128, MT * 128], BF16)
    w2s = sb3("w2s", [128, KMID * 128], BF16)
    ofe3 = sb3("ofe3", [128, MT * GRP])
    otk3 = sb3("otk3", [128, 4 * DM])
    u1t3 = sb3("u1t3", [128, 2 * DM])
    junkb = sb3("junkb", [128, DM], BF16)
    junkf = sb3("junkf", [128, DM])

    p.op("v", lambda v: v.memset(gmaxt[:, 0:1], 0.0), writes=[gmaxt])
    p.op("v", lambda v: v.memset(gmaxt[:, 2:3], 0.0), writes=[gmaxt])
    for g_ in range(NGRP):
        for k in range(MT):
            p.op("q", lambda q, g_=g_, k=k: q.dma_start(
                out=u2g[:, k * GRP:(k + 1) * GRP],
                in_=scr_u2f.h[k, :, g_ * GRP:(g_ + 1) * GRP]),
                reads=[scr_u2f], writes=[u2g], dma=True)
        for m in range(KMID):
            p.op("q", lambda q, m=m: q.dma_start(
                out=w1s[:, :], in_=scr_w1T.h[m]), reads=[scr_w1T], writes=[w1s], dma=True)
            for j in range(JPG):
                pb = pbA if (m + j) % 2 == 0 else pbB
                items = []
                for k in range(MT):
                    def em(t, m=m, j=j, k=k, pb=pb, first=(k == 0), last=(k == MT - 1)):
                        return t.matmul(pb[:, 0:JW],
                                        w1s[:, k * 128:(k + 1) * 128],
                                        u2g[:, k * GRP + j * JW:k * GRP + j * JW + JW],
                                        start=first, stop=last)
                    items.append((em, [w1s, u2g]))
                mm_group(p, pb, items)
                p.op("s", lambda s, m=m, j=j, pb=pb: s.activation(
                    fgrp[:, m * GRP + j * JW:m * GRP + j * JW + JW], pb[:, 0:JW],
                    AF.Relu, bias=cf1[:, m:m + 1]), reads=[pb, cf1], writes=[fgrp])
        for m in range(MT):
            p.op("q", lambda q, m=m: q.dma_start(
                out=w2s[:, :], in_=scr_w2T.h[m]), reads=[scr_w2T], writes=[w2s], dma=True)
            for j in range(JPG):
                pb = pbC if (m + j) % 2 == 0 else pbD
                items = []
                for k in range(KMID):
                    def em(t, m=m, j=j, k=k, pb=pb, first=(k == 0), last=(k == KMID - 1)):
                        return t.matmul(pb[:, 0:JW],
                                        w2s[:, k * 128:(k + 1) * 128],
                                        fgrp[:, k * GRP + j * JW:k * GRP + j * JW + JW],
                                        start=first, stop=last)
                    items.append((em, [w2s, fgrp]))
                mm_group(p, pb, items)
                p.op("s", lambda s, m=m, j=j, pb=pb: s.activation(
                    ofe3[:, m * GRP + j * JW:m * GRP + j * JW + JW], pb[:, 0:JW],
                    AF.Identity, bias=bf2t[:, m:m + 1]), reads=[pb, bf2t], writes=[ofe3])
        for j in range(JPG):
            for i in range(JI):
                for m in range(MT):
                    p.op("t", lambda t, m=m, j=j, i=i: t.transpose(
                        ptF[:, 0:128],
                        ofe3[:, m * GRP + j * JW + i * 128:m * GRP + j * JW + i * 128 + 128],
                        idf[:, :]), reads=[ofe3, idf], writes=[ptF])
                    p.op("s", lambda s, m=m, i=i: s.copy(
                        otk3[:, i * DM + m * 128:i * DM + m * 128 + 128], ptF[:, 0:128]),
                        reads=[ptF], writes=[otk3])
            for i in range(JI):
                row0 = g_ * GRP + j * JW + i * 128
                p.op("q", lambda q, row0=row0: q.dma_start(
                    out=u1t3[:, 0:DM], in_=scr_u1.h[row0:row0 + 128, :]),
                    reads=[scr_u1], writes=[u1t3], dma=True)
                p.op("q", lambda q, row0=row0: q.dma_start(
                    out=u1t3[:, DM:2 * DM], in_=xmine[row0:row0 + 128, :]),
                    reads=[xmine], writes=[u1t3], dma=True)
                p.op("v", lambda v, i=i: v.tensor_tensor(
                    otk3[:, i * DM:(i + 1) * DM], otk3[:, i * DM:(i + 1) * DM],
                    u1t3[:, 0:DM], ALU.add), reads=[otk3, u1t3], writes=[otk3])
                p.op("v", lambda v, i=i: v.tensor_tensor(
                    otk3[:, i * DM:(i + 1) * DM], otk3[:, i * DM:(i + 1) * DM],
                    u1t3[:, DM:2 * DM], ALU.add), reads=[otk3, u1t3], writes=[otk3])
                p.op("q", lambda q, row0=row0, i=i: q.dma_start(
                    out=scr_xp.h[row0:row0 + 128, :], in_=otk3[:, i * DM:(i + 1) * DM]),
                    reads=[otk3], writes=[scr_xp], dma=True)
            ln_stats(otk3, JI, 32, junkb, junkf)
            for i in range(JI):
                trow = (g_ * GRP + j * JW) // 128 + i
                p.op("v", lambda v, trow=trow, i=i: v.tensor_copy(
                    lnos[:, trow:trow + 1], st1[:, 40 + i:41 + i]),
                    reads=[st1], writes=[lnos])
                p.op("v", lambda v, trow=trow, i=i: v.tensor_copy(
                    lnos[:, NTT + trow:NTT + trow + 1], st1[:, 48 + i:49 + i]),
                    reads=[st1], writes=[lnos])
                p.op("s", lambda s, i=i: s.activation(
                    junkf[:, 0:DM], otk3[:, i * DM:(i + 1) * DM], AF.Identity,
                    scale=st1[:, 40 + i:41 + i], bias=st1[:, 48 + i:49 + i]),
                    reads=[otk3, st1], writes=[junkf])
                p.op("v", lambda v: v.tensor_reduce(
                    gmaxt[:, 1:2], junkf[:, 0:DM], AXX, ALU.max,
                    apply_absolute_value=True), reads=[junkf], writes=[gmaxt])
                p.op("v", lambda v: v.tensor_tensor(
                    gmaxt[:, 0:1], gmaxt[:, 0:1], gmaxt[:, 1:2], ALU.max),
                    reads=[gmaxt], writes=[gmaxt])

    def global_max_and_bcast(local_col, bounce_in, bounce_out, dst_col):
        p.op("t", lambda t: t.transpose(ptF[0:1, 0:128], gmaxt[:, local_col:local_col + 1],
                                        idf[:, :]), reads=[gmaxt, idf], writes=[ptF])
        p.op("s", lambda s: s.copy(vt1[0:1, 0:128], ptF[0:1, 0:128]),
             reads=[ptF], writes=[vt1])
        p.op("v", lambda v: v.tensor_reduce(
            vt1[0:1, 128:129], vt1[0:1, 0:128], AXX, ALU.max),
            reads=[vt1], writes=[vt1])
        p.op("q", lambda q: q.dma_start(out=gmaxt[0:1, 3:4], in_=vt1[0:1, 128:129],
                                        allow_slow_non_contiguous=True),
             reads=[vt1], writes=[gmaxt], dma=True)
        nrep = 1
        while nrep < 128:
            p.op("q", lambda q, n=nrep: q.dma_start(
                out=gmaxt[n:2 * n, 3:4], in_=gmaxt[0:n, 3:4],
                allow_slow_non_contiguous=True),
                reads=[gmaxt], writes=[gmaxt], dma=True)
            nrep *= 2
        p.op("q", lambda q: q.dma_start(out=bounce_in.h[:, :], in_=gmaxt[:, 3:4],
                                        allow_slow_non_contiguous=True),
             reads=[gmaxt], writes=[bounce_in], dma=True)
        p.op("g", lambda g: g.collective_compute(
            "AllReduce", ALU.max, replica_groups=ALL8,
            ins=[bounce_in.h[:, :].opt()], outs=[bounce_out.h[:, :].opt()]),
            reads=[bounce_in], writes=[bounce_out], coll=True)
        p.op("q", lambda q, dst_col=dst_col: q.dma_start(
            out=qsc[:, dst_col:dst_col + 1], in_=bounce_out.h[:, :],
            allow_slow_non_contiguous=True),
            reads=[bounce_out], writes=[qsc], dma=True)

    global_max_and_bcast(0, scr_mx, scr_mxo, 0)
    p.op("g", lambda g: g.dma_start(
        out=clt[0:1, 0:1], in_=clip_x[0:1], allow_slow_non_contiguous=True),
        reads=[clip_x], writes=[clt], dma=True)
    p.op("g", lambda g: g.dma_start(
        out=clt[0:1, 1:2], in_=clip_down[0:1], allow_slow_non_contiguous=True),
        reads=[clip_down], writes=[clt], dma=True)
    nrep = 1
    while nrep < 128:
        p.op("q", lambda q, n=nrep: q.dma_start(
            out=clt[n:2 * n, 0:2], in_=clt[0:n, 0:2],
            allow_slow_non_contiguous=True),
            reads=[clt], writes=[clt], dma=True)
        nrep *= 2
    p.op("s", lambda s: s.activation(qsc[:, 2:4], clt[:, 0:2], AF.Exp, scale=-1.0),
         reads=[clt], writes=[qsc])
    p.op("v", lambda v: v.tensor_scalar(qsc[:, 2:4], qsc[:, 2:4], 1.0, None, ALU.add),
         reads=[qsc], writes=[qsc])
    p.op("v", lambda v: v.reciprocal(qsc[:, 2:4], qsc[:, 2:4]),
         reads=[qsc], writes=[qsc])
    p.op("v", lambda v: v.scalar_tensor_tensor(
        qsc[:, 0:1], qsc[:, 0:1], 1.0 / QMAX, qsc[:, 2:3], ALU.mult, ALU.mult),
        reads=[qsc], writes=[qsc])
    p.op("v", lambda v: v.reciprocal(qsc[:, 4:5], qsc[:, 0:1]),
         reads=[qsc], writes=[qsc])
    p.op("v", lambda v: v.tensor_scalar(
        scg[:, :], scg[:, :], qsc[:, 0:1], None, ALU.mult), reads=[scg, qsc], writes=[scg])
    p.op("v", lambda v: v.tensor_scalar(
        scu[:, :], scu[:, :], qsc[:, 0:1], None, ALU.mult), reads=[scu, qsc], writes=[scu])

    xqf = sb3("xqf", [128, MT * GRP], BF16)
    wgs = sb3("wgs", [128, MT * 128], BF16)
    wus = sb3("wus", [128, MT * 128], BF16)
    hg1 = sb3("hg1", [128, 512])
    hg2 = sb3("hg2", [128, 512])
    qv1 = sb3("qv1", [128, DM])
    qv2 = sb3("qv2", [128, DM])
    qvi = sb3("qvi", [128, DM], I32)
    qvb = sb3("qvb", [128, DM], BF16)

    def quant_chain(width, src_emit, src_reads, dst_tile, dst_slice, inv_col,
                    src_stream="s", src_dma=False):
        p.op(src_stream, src_emit, reads=src_reads, writes=[qv1], dma=src_dma)
        p.op("v", lambda v: v.tensor_scalar(
            qv1[:, 0:width], qv1[:, 0:width], qsc[:, inv_col:inv_col + 1], 1e6,
            ALU.mult, ALU.min), reads=[qv1, qsc], writes=[qv1])
        p.op("v", lambda v: v.tensor_scalar(
            qv1[:, 0:width], qv1[:, 0:width], -1e6, None, ALU.max),
            reads=[qv1], writes=[qv1])
        p.op("v", lambda v: v.tensor_copy(qvi[:, 0:width], qv1[:, 0:width]),
             reads=[qv1], writes=[qvi])
        p.op("v", lambda v: v.tensor_copy(qv2[:, 0:width], qvi[:, 0:width]),
             reads=[qvi], writes=[qv2])
        p.op("v", lambda v: v.tensor_tensor(
            qv1[:, 0:width], qv1[:, 0:width], qv2[:, 0:width], ALU.subtract),
            reads=[qv1, qv2], writes=[qv1])
        p.op("v", lambda v: v.scalar_tensor_tensor(
            qv2[:, 0:width], qv1[:, 0:width], 0.5, qv2[:, 0:width], ALU.is_gt, ALU.add),
            reads=[qv1, qv2], writes=[qv2])
        p.op("v", lambda v: v.scalar_tensor_tensor(
            qv2[:, 0:width], qv1[:, 0:width], -0.5, qv2[:, 0:width], ALU.is_lt,
            ALU.subtract), reads=[qv1, qv2], writes=[qv2])
        p.op("v", lambda v: v.tensor_scalar(
            dst_tile[:, dst_slice[0]:dst_slice[1]], qv2[:, 0:width],
            -QMAX - 1.0, QMAX, ALU.max, ALU.min),
            reads=[qv2], writes=[dst_tile])

    for g_ in range(NGRP):
        for i in range(GRP // 128):
            row0 = g_ * GRP + i * 128
            trow = row0 // 128
            p.op("q", lambda q, row0=row0: q.dma_start(
                out=u1t3[:, 0:DM], in_=scr_xp.h[row0:row0 + 128, :]),
                reads=[scr_xp], writes=[u1t3], dma=True)
            quant_chain(
                DM,
                lambda s, trow=trow: s.activation(
                    qv1[:, 0:DM], u1t3[:, 0:DM], AF.Identity,
                    scale=lnos[:, trow:trow + 1],
                    bias=lnos[:, NTT + trow:NTT + trow + 1]),
                [u1t3, lnos], qvb, (0, DM), 4)
            for k in range(MT):
                p.op("t", lambda t, k=k: t.transpose(
                    ptB[:, 0:128], qvb[:, k * 128:(k + 1) * 128], idn[:, :]),
                    reads=[qvb, idn], writes=[ptB])
                p.op("s", lambda s, k=k, i=i: s.copy(
                    xqf[:, k * GRP + i * 128:k * GRP + i * 128 + 128], ptB[:, 0:128]),
                    reads=[ptB], writes=[xqf])
        for m in range(KQ):
            p.op("q", lambda q, m=m: q.dma_start(
                out=wgs[:, :], in_=scr_wgT.h[m]), reads=[scr_wgT], writes=[wgs], dma=True)
            p.op("q", lambda q, m=m: q.dma_start(
                out=wus[:, :], in_=scr_wuT.h[m]), reads=[scr_wuT], writes=[wus], dma=True)
            for j in range(JPG):
                pb = pbA if j % 2 == 0 else pbB
                items = []
                for k in range(MT):
                    def em(t, m=m, j=j, k=k, pb=pb, first=(k == 0), last=(k == MT - 1)):
                        return t.matmul(pb[:, 0:JW],
                                        wgs[:, k * 128:(k + 1) * 128],
                                        xqf[:, k * GRP + j * JW:k * GRP + j * JW + JW],
                                        start=first, stop=last)
                    items.append((em, [wgs, xqf]))
                mm_group(p, pb, items)
                p.op("s", lambda s, m=m, pb=pb: s.activation(
                    hg1[:, 0:JW], pb[:, 0:JW], AF.Silu, scale=scg[:, m:m + 1]),
                    reads=[pb, scg], writes=[hg1])
                pb2 = pbC if j % 2 == 0 else pbD
                items = []
                for k in range(MT):
                    def em(t, m=m, j=j, k=k, pb2=pb2, first=(k == 0), last=(k == MT - 1)):
                        return t.matmul(pb2[:, 0:JW],
                                        wus[:, k * 128:(k + 1) * 128],
                                        xqf[:, k * GRP + j * JW:k * GRP + j * JW + JW],
                                        start=first, stop=last)
                    items.append((em, [wus, xqf]))
                mm_group(p, pb2, items)
                p.op("s", lambda s, m=m, pb2=pb2: s.activation(
                    hg2[:, 0:JW], pb2[:, 0:JW], AF.Identity, scale=scu[:, m:m + 1]),
                    reads=[pb2, scu], writes=[hg2])
                p.op("v", lambda v: v.tensor_tensor(
                    hg1[:, 0:JW], hg1[:, 0:JW], hg2[:, 0:JW], ALU.mult),
                    reads=[hg1, hg2], writes=[hg1])
                p.op("v", lambda v: v.tensor_reduce(
                    gmaxt[:, 1:2], hg1[:, 0:JW], AXX, ALU.max,
                    apply_absolute_value=True), reads=[hg1], writes=[gmaxt])
                p.op("v", lambda v: v.tensor_tensor(
                    gmaxt[:, 2:3], gmaxt[:, 2:3], gmaxt[:, 1:2], ALU.max),
                    reads=[gmaxt], writes=[gmaxt])
                p.op("q", lambda q, g_=g_, m=m, j=j: q.dma_start(
                    out=scr_hpre.h[m, :, g_ * GRP + j * JW:g_ * GRP + j * JW + JW],
                    in_=hg1[:, 0:JW]), reads=[hg1], writes=[scr_hpre], dma=True)

    global_max_and_bcast(2, scr_mh, scr_mho, 1)
    p.op("v", lambda v: v.scalar_tensor_tensor(
        qsc[:, 1:2], qsc[:, 1:2], 1.0 / QMAX, qsc[:, 3:4], ALU.mult, ALU.mult),
        reads=[qsc], writes=[qsc])
    p.op("v", lambda v: v.reciprocal(qsc[:, 5:6], qsc[:, 1:2]),
         reads=[qsc], writes=[qsc])
    p.op("v", lambda v: v.tensor_scalar(
        scdt[:, :], scdt[:, :], qsc[:, 1:2], None, ALU.mult),
        reads=[scdt, qsc], writes=[scdt])

    hqf = sb3("hqf", [128, KQ * 512], BF16)
    wds = sb3("wds", [128, KQ * 128], BF16)
    for g_ in range(NGRP):
        for j in range(JPG):
            col0 = g_ * GRP + j * JW
            for m in range(KQ):
                quant_chain(
                    JW,
                    lambda q, m=m, col0=col0: q.dma_start(
                        out=qv1[:, 0:JW],
                        in_=bass.AP(scr_hpre.h.tensor,
                                    scr_hpre.h.offset + m * 128 * MYT + col0,
                                    [[MYT, 128], [1, JW]])),
                    [scr_hpre], hqf, (m * 512, m * 512 + JW), 5,
                    src_stream="q", src_dma=True)
            for m in range(MT):
                p.op("q", lambda q, m=m: q.dma_start(
                    out=wds[:, :], in_=scr_wdT.h[m]), reads=[scr_wdT], writes=[wds],
                    dma=True)
                pb = pbA if m % 2 == 0 else pbB
                items = []
                for k in range(KQ):
                    def em(t, m=m, k=k, pb=pb, first=(k == 0), last=(k == KQ - 1)):
                        return t.matmul(pb[:, 0:JW],
                                        wds[:, k * 128:(k + 1) * 128],
                                        hqf[:, k * 512:k * 512 + JW],
                                        start=first, stop=last)
                    items.append((em, [wds, hqf]))
                mm_group(p, pb, items)
                p.op("s", lambda s, m=m, pb=pb: s.activation(
                    ofe3[:, m * JW:m * JW + JW], pb[:, 0:JW], AF.Identity,
                    scale=scdt[:, m:m + 1]), reads=[pb, scdt], writes=[ofe3])
            for i in range(JI):
                for m in range(MT):
                    p.op("t", lambda t, m=m, i=i: t.transpose(
                        ptF[:, 0:128],
                        ofe3[:, m * JW + i * 128:m * JW + i * 128 + 128],
                        idf[:, :]), reads=[ofe3, idf], writes=[ptF])
                    p.op("s", lambda s, m=m, i=i: s.copy(
                        otk3[:, i * DM + m * 128:i * DM + m * 128 + 128], ptF[:, 0:128]),
                        reads=[ptF], writes=[otk3])
            for i in range(JI):
                row0 = col0 + i * 128
                p.op("q", lambda q, row0=row0: q.dma_start(
                    out=u1t3[:, 0:DM], in_=scr_xp.h[row0:row0 + 128, :]),
                    reads=[scr_xp], writes=[u1t3], dma=True)
                p.op("v", lambda v, i=i: v.tensor_tensor(
                    otk3[:, i * DM:(i + 1) * DM], otk3[:, i * DM:(i + 1) * DM],
                    u1t3[:, 0:DM], ALU.add), reads=[otk3, u1t3], writes=[otk3])
                p.op("q", lambda q, row0=row0, i=i: q.dma_start(
                    out=out.h[row0:row0 + 128, :], in_=otk3[:, i * DM:(i + 1) * DM]),
                    reads=[otk3], writes=[out], dma=True)

    p.wait("q", out.last_w)
    p.emit()
    es_p3.close()
    es.close()
    return nc


_NC_CACHE = {}


def kernel(L0_attn_x, L0_attn_in_proj_w, L0_attn_conv_w, L0_attn_conv_b,
           L0_attn_x_proj_w, L0_attn_dt_proj_w, L0_attn_dt_proj_b, L0_attn_A_log,
           L0_attn_D, L0_attn_out_proj_w, L0_attn_ln1_g, L0_attn_ln1_b,
           L0_attn_ln2_g, L0_attn_ln2_b, L0_attn_ffn_w1, L0_attn_ffn_b1,
           L0_attn_ffn_w2, L0_attn_ffn_b2, L0_ffn_clip_x, L0_ffn_clip_down,
           L0_ffn_w_gate, L0_ffn_w_up, L0_ffn_w_down):
    x = np.ascontiguousarray(np.asarray(L0_attn_x, dtype=np.float32))
    B, T, _ = x.shape
    TKN = T
    MYT = T // 2
    TC = 256
    NCH = TKN // TC

    f32 = lambda a: np.ascontiguousarray(np.asarray(a, dtype=np.float32))
    w_in = f32(L0_attn_in_proj_w)
    w_cv = f32(L0_attn_conv_w)[:, 0, :]
    b_cv = f32(L0_attn_conv_b)
    w_xp = f32(L0_attn_x_proj_w)
    w_dt = f32(L0_attn_dt_proj_w)
    b_dt = f32(L0_attn_dt_proj_b)
    a_lg = f32(L0_attn_A_log)
    d_v = f32(L0_attn_D)
    w_op = f32(L0_attn_out_proj_w)

    key = (TKN, TC)
    if key not in _NC_CACHE:
        _NC_CACHE[key] = build(TKN=TKN, TC=TC)
    nc = _NC_CACHE[key]

    in_maps = []
    for core in range(8):
        b = core // 2
        h = core % 2
        sl = slice(h * 1024, (h + 1) * 1024)
        im = {
            "xfull": x[b],
            "xmine": np.ascontiguousarray(
                x[b].reshape(NCH, 2, TC // 2, DM)[:, h].reshape(MYT, DM)),
            "w_inproj": np.ascontiguousarray(
                np.concatenate([w_in[sl], w_in[2048 + h * 1024: 2048 + (h + 1) * 1024]], 0)),
            "w_conv": np.ascontiguousarray(w_cv[sl]),
            "b_conv": np.ascontiguousarray(b_cv[sl]),
            "w_xproj": np.ascontiguousarray(w_xp[:, sl]),
            "w_dtproj": np.ascontiguousarray(w_dt[sl]),
            "b_dtproj": np.ascontiguousarray(b_dt[sl]),
            "alog": np.ascontiguousarray(a_lg[sl]),
            "dvec": np.ascontiguousarray(d_v[sl]),
            "w_outproj": np.ascontiguousarray(w_op[:, sl]),
            "ln1_g": f32(L0_attn_ln1_g), "ln1_b": f32(L0_attn_ln1_b),
            "ln2_g": f32(L0_attn_ln2_g), "ln2_b": f32(L0_attn_ln2_b),
            "w_ffn1": f32(L0_attn_ffn_w1), "b_ffn1": f32(L0_attn_ffn_b1),
            "w_ffn2": f32(L0_attn_ffn_w2), "b_ffn2": f32(L0_attn_ffn_b2),
            "clip_x": f32(L0_ffn_clip_x), "clip_down": f32(L0_ffn_clip_down),
            "w_gate": f32(L0_ffn_w_gate), "w_up": f32(L0_ffn_w_up),
            "w_down": f32(L0_ffn_w_down),
        }
        in_maps.append(im)

    res = run_bass_kernel_spmd(nc, in_maps, core_ids=list(range(8)))

    y = np.empty((B, T, DM), dtype=np.float32)
    for core in range(8):
        b = core // 2
        h = core % 2
        o = np.asarray(res.results[core]["out"]).reshape(NCH, TC // 2, DM)
        yv = y[b].reshape(NCH, 2, TC // 2, DM)
        yv[:, h] = o
    return y


# version 05
# speedup vs baseline: 1.1280x; 1.1280x over previous
import bisect
import numpy as np
from contextlib import ExitStack

import concourse.bass as bass
import concourse.mybir as mybir
from concourse.bass_utils import run_bass_kernel_spmd

F32 = mybir.dt.float32
BF16 = mybir.dt.bfloat16
I32 = mybir.dt.int32
AF = mybir.ActivationFunctionType
ALU = mybir.AluOpType
AXX = mybir.AxisListType.X

DM = 1024
DH = 1024
NST = 16
RNK = 64
DMID = 4096
DQ = 2816
KQ = DQ // 128
KT = DH // 128
MT = DM // 128
KMID = DMID // 128
EPS = 1e-5
QMAX = 127.0
PAIRS = [[0, 1], [2, 3], [4, 5], [6, 7]]
ALL8 = [list(range(8))]


class Tile:
    __slots__ = ("name", "h", "last_w", "readers")

    def __init__(self, name, h):
        self.name = name
        self.h = h
        self.last_w = None
        self.readers = []

    def __getitem__(self, key):
        return self.h[key]


class Prog:
    STREAMS = ("t", "v", "s", "g", "q")
    NQS = 12
    DOMS = (["T", "V", "S", "G", "C"]
            + [f"D{i}" for i in range(12)] + [f"Q{i}" for i in range(12)])

    def __init__(self, nc, es):
        self.nc = nc
        self.stream_ops = {st: [] for st in self.STREAMS}
        self.val = {d: 0 for d in self.DOMS}
        self.sem = {d: es.enter_context(nc.semaphore("sem" + d)) for d in self.DOMS}
        self.hist = {d: ([], []) for d in self.DOMS}
        self.sclock = {st: {d: 0 for d in self.DOMS} for st in self.STREAMS}
        self.dma_idx = {"D": 0, "Q": 0}
        self.n_ops = 0

    def _dom_for(self, st, dma, coll):
        if st == "q":
            q = self.dma_idx["Q"]; self.dma_idx["Q"] += 1
            return f"Q{q % self.NQS}"
        if st == "g":
            if coll:
                return "C"
            if dma:
                q = self.dma_idx["D"]; self.dma_idx["D"] += 1
                return f"D{q % self.NQS}"
            return "G"
        return {"t": "T", "v": "V", "s": "S"}[st]

    def _clock_of(self, d, v):
        vals, clocks = self.hist[d]
        i = bisect.bisect_left(vals, v)
        assert i < len(vals) and vals[i] == v, (d, v)
        return clocks[i]

    def op(self, st, emit, reads=(), writes=(), dma=False, coll=False,
           war_writes=(), track=True):
        dom = self._dom_for(st, dma, coll)
        is_dma = dom[0] in ("D", "Q") and len(dom) > 1
        inc = 16 if is_dma else 1
        clk = self.sclock[st]
        deps = {}

        def need(dep):
            if dep is None:
                return
            dd, vv = dep
            if clk[dd] >= vv:
                return
            if deps.get(dd, 0) < vv:
                deps[dd] = vv

        for tl in reads:
            need(tl.last_w)
        for tl in list(writes) + list(war_writes):
            need(tl.last_w)
            for r in tl.readers:
                need(r)
        if is_dma and self.val[dom] > 0:
            need((dom, self.val[dom]))
        for dd, vv in sorted(deps.items()):
            c2 = self._clock_of(dd, vv)
            for k2, v2 in c2.items():
                if clk[k2] < v2:
                    clk[k2] = v2
            if clk[dd] < vv:
                clk[dd] = vv
        myval = self.val[dom] + inc
        self.val[dom] = myval
        snap = dict(clk)
        snap[dom] = max(snap.get(dom, 0), myval)
        self.hist[dom][0].append(myval)
        self.hist[dom][1].append(snap)

        dep_list = sorted(deps.items())
        sem = self.sem
        semd = sem[dom]

        def emit_fn(eng, _emit=emit, _deps=dep_list, _semd=semd, _inc=inc):
            for dd, vv in _deps[:-1]:
                eng.wait_ge(sem[dd], vv)
            ins = _emit(eng)
            if _deps:
                ins._wait_ge(sem[_deps[-1][0]], _deps[-1][1])
            ins.then_inc(_semd, _inc)
            return ins

        self.stream_ops[st].append(emit_fn)
        self.n_ops += 1
        tok = (dom, myval)
        if track:
            for tl in reads:
                tl.readers.append(tok)
            for tl in writes:
                tl.last_w = tok
                tl.readers = []
        return tok

    def wait(self, st, dep):
        if dep is None:
            return
        dd, vv = dep
        clk = self.sclock[st]
        if clk[dd] >= vv:
            return
        c2 = self._clock_of(dd, vv)
        for k2, v2 in c2.items():
            if clk[k2] < v2:
                clk[k2] = v2
        if clk[dd] < vv:
            clk[dd] = vv
        sem = self.sem

        def emit_fn(eng, _dd=dd, _vv=vv):
            eng.wait_ge(sem[_dd], _vv)
        self.stream_ops[st].append(emit_fn)

    def barrier(self):
        vals = dict(self.val)
        for st in self.STREAMS:
            for d, v in vals.items():
                if v > 0:
                    self.wait(st, (d, v))

    def emit(self):
        so = self.stream_ops
        with self.nc.Block() as block:
            @block.sync
            def _(eng):
                for f in so["q"]:
                    f(eng)

            @block.vector
            def _(eng):
                for f in so["v"]:
                    f(eng)

            @block.scalar
            def _(eng):
                for f in so["s"]:
                    f(eng)

            @block.gpsimd
            def _(eng):
                for f in so["g"]:
                    f(eng)

            @block.tensor
            def _(eng):
                for f in so["t"]:
                    f(eng)


def mk_slots(tile, n):
    return [Tile(f"{tile.name}~{j}", tile.h) for j in range(n)]


def mm_group(p, psum_tile, items):
    n = len(items)
    tok = None
    for i, (emit, reads) in enumerate(items):
        first = i == 0
        last = i == n - 1
        tok = p.op("t", emit, reads=list(reads),
                   writes=[psum_tile] if last else (),
                   war_writes=[psum_tile] if (first and not last) else ())
    return tok


def build(TKN=8192, TC=512):
    NCH = TKN // TC
    MYT = TKN // 2
    MYC = TC // 2
    TT = TC // 128
    NTT = MYT // 128
    GRP = min(512, MYT)
    NGRP = MYT // GRP
    JPG = GRP // 512 if GRP >= 512 else 1
    JW = min(512, GRP)
    JI = JW // 128

    nc = bass.Bass("TRN2", target_bir_lowering=False)
    es = ExitStack()
    p = Prog(nc, es)

    def din(name, shape, dt=F32):
        return Tile(name, nc.dram_tensor(name, shape, dt, kind="ExternalInput").ap())

    def dscr(name, shape, dt):
        return Tile(name, nc.dram_tensor(name, shape, dt).ap())

    def mksb(stack):
        def sb(name, shape, dt=F32):
            return Tile(name, stack.enter_context(nc.sbuf_tensor(name, shape, dt)))
        return sb

    sbP = mksb(es)

    def pstP(name, shape, dt=F32):
        return Tile(name, es.enter_context(nc.psum_tensor(name, shape, dt)))

    xfull = din("xfull", [TKN, DM])
    xmine = din("xmine", [MYT, DM])
    w_inproj = din("w_inproj", [2 * DH, DM])
    w_conv = din("w_conv", [DH, 4])
    b_conv = din("b_conv", [DH])
    w_xproj = din("w_xproj", [96, DH])
    w_dtproj = din("w_dtproj", [DH, RNK])
    b_dtproj = din("b_dtproj", [DH])
    alog = din("alog", [DH, NST])
    dvec = din("dvec", [DH])
    w_outproj = din("w_outproj", [DM, DH])
    ln1_g = din("ln1_g", [DM]); ln1_b = din("ln1_b", [DM])
    ln2_g = din("ln2_g", [DM]); ln2_b = din("ln2_b", [DM])
    w_ffn1 = din("w_ffn1", [DMID, DM]); b_ffn1 = din("b_ffn1", [DMID])
    w_ffn2 = din("w_ffn2", [DM, DMID]); b_ffn2 = din("b_ffn2", [DM])
    clip_x = din("clip_x", [1]); clip_down = din("clip_down", [1])
    w_gate = din("w_gate", [DQ, DM])
    w_up = din("w_up", [DQ, DM])
    w_down = din("w_down", [DM, DQ])
    out = Tile("out", nc.dram_tensor("out", [MYT, DM], F32, kind="ExternalOutput").ap())

    scr_xdin = dscr("scr_xdin", [NCH, 96, TC], F32)
    scr_xdout = dscr("scr_xdout", [NCH, 96, TC], F32)
    scr_rsin = dscr("scr_rsin", [NCH, TC, DM], F32)
    scr_rsout = dscr("scr_rsout", [NCH, MYC, DM], F32)
    scr_u1 = dscr("scr_u1", [MYT, DM], F32)
    scr_u2f = dscr("scr_u2f", [MT, 128, MYT], BF16)
    scr_xp = dscr("scr_xp", [MYT, DM], F32)
    scr_hpre = dscr("scr_hpre", [KQ, 128, MYT], F32)
    scr_woT = dscr("scr_woT", [MT, 128, KT * 128], BF16)
    scr_wiT = dscr("scr_wiT", [16, 128, MT * 128], BF16)
    scr_wcv = dscr("scr_wcv", [KT, 128, 4 * 128], BF16)
    scr_w1T = dscr("scr_w1T", [KMID, 128, MT * 128], BF16)
    scr_w2T = dscr("scr_w2T", [MT, 128, KMID * 128], BF16)
    scr_wgT = dscr("scr_wgT", [KQ, 128, MT * 128], BF16)
    scr_wuT = dscr("scr_wuT", [KQ, 128, MT * 128], BF16)
    scr_wdT = dscr("scr_wdT", [MT, 128, KQ * 128], BF16)
    scr_mx = dscr("scr_mx", [128, 1], F32)
    scr_mxo = dscr("scr_mxo", [128, 1], F32)
    scr_mh = dscr("scr_mh", [128, 1], F32)
    scr_mho = dscr("scr_mho", [128, 1], F32)

    wxt = sbP("wxt", [128, KT * 96], F32)
    wdtp = sbP("wdtp", [64, KT * 128], BF16)
    asbn = sbP("asbn", [128, KT * NST])
    dsb = sbP("dsb", [128, KT])
    cbs = sbP("cbs", [128, KT])
    dtbv = sbP("dtbv", [128, KT])
    wb1 = sbP("wb1", [128, 16])
    cf1 = sbP("cf1", [128, KMID])
    bf2t = sbP("bf2t", [128, MT])
    scg = sbP("scg", [128, KQ])
    scu = sbP("scu", [128, KQ])
    scdt = sbP("scdt", [128, MT])
    idn = sbP("idn", [128, 128], BF16)
    idf = sbP("idf", [128, 128], F32)
    qsc = sbP("qsc", [128, 8])
    lnos = sbP("lnos", [128, 2 * NTT])
    gmaxt = sbP("gmaxt", [128, 4])
    clt = sbP("clt", [128, 2])
    st1 = sbP("st1", [128, 64])
    cns = sbP("cns", [128, 4])
    g2t = sbP("g2t", [128, MT])
    vt1 = sbP("vt1", [128, 2 * 512])
    vt2 = sbP("vt2", [128, 2 * 512])
    gt1 = sbP("gt1", [128, 512])
    xdbt = sbP("xdbt", [96, 512])
    xdbb = sbP("xdbb", [64, 512], BF16)

    pbA = pstP("pbA", [128, 512])
    pbB = pstP("pbB", [128, 512])
    pbC = pstP("pbC", [128, 512])
    pbD = pstP("pbD", [128, 512])
    pbY = pstP("pbY", [128, 512])
    pbX = pstP("pbX", [96, 512])
    ptB = pstP("ptB", [128, 1024], BF16)
    ptF = pstP("ptF", [128, 512], F32)

    es_pp = ExitStack()
    sbp = mksb(es_pp)
    io1 = sbp("io1", [128, 128])
    io2 = sbp("io2", [128, 128])
    p.op("v", lambda v: v.memset(cns[:, 0:1], EPS), writes=[cns])
    p.op("v", lambda v: v.memset(cns[:, 1:2], -0.5), writes=[cns])
    p.op("v", lambda v: v.memset(cns[:, 2:3], 0.5), writes=[cns])
    p.op("v", lambda v: v.memset(cns[:, 3:4], 0.5), writes=[cns])
    p.op("g", lambda g: g.iota(io1[:, :], [[1, 128]], channel_multiplier=0,
                               allow_small_or_imprecise_dtypes=True), writes=[io1])
    p.op("g", lambda g: g.iota(io2[:, :], [[0, 128]], channel_multiplier=1,
                               allow_small_or_imprecise_dtypes=True), writes=[io2])
    p.op("v", lambda v: v.tensor_tensor(idn[:, :], io1[:, :], io2[:, :], ALU.is_equal),
         reads=[io1, io2], writes=[idn])
    p.op("v", lambda v: v.tensor_tensor(idf[:, :], io1[:, :], io2[:, :], ALU.is_equal),
         reads=[io1, io2], writes=[idf])

    def load_vec(dst, src, n, col0=0):
        kk = n // 128
        p.op("g", lambda g, _c=col0, _k=kk: g.dma_start(
            out=dst[:, _c:_c + _k],
            in_=bass.AP(src.h.tensor, src.h.offset, [[1, 128], [128, _k]]),
            allow_slow_non_contiguous=True), reads=[src], writes=[dst], dma=True)

    load_vec(dsb, dvec, DH)
    load_vec(cbs, b_conv, DH)
    load_vec(dtbv, b_dtproj, DH)
    g1t = sbp("g1t", [128, MT]); b1tt = sbp("b1tt", [128, MT], BF16)
    b2tt = sbp("b2tt", [128, MT], BF16)
    bf1t = sbp("bf1t", [128, KMID])
    load_vec(g1t, ln1_g, DM)
    load_vec(b1tt, ln1_b, DM)
    load_vec(g2t, ln2_g, DM)
    load_vec(b2tt, ln2_b, DM)
    load_vec(bf2t, b_ffn2, DM)
    load_vec(bf1t, b_ffn1, DMID)

    atmp = sbp("atmp", [128, KT * NST])
    for k in range(KT):
        p.op("g", lambda g, k=k: g.dma_start(
            out=atmp[:, k * NST:(k + 1) * NST], in_=alog[k * 128:(k + 1) * 128, :]),
            reads=[alog], writes=[atmp], dma=True)
    p.op("s", lambda s: s.activation(asbn[:, :], atmp[:, :], AF.Exp),
         reads=[atmp], writes=[asbn])
    p.op("v", lambda v: v.tensor_scalar(asbn[:, :], asbn[:, :], -1.0, None, ALU.mult),
         reads=[asbn], writes=[asbn])

    wstage = sbp("wstage", [128, 8192], BF16)
    wloadf = sbp("wloadf", [128, DQ])
    wqi = sbp("wqi", [128, DM], I32)
    wq1 = sbp("wq1", [128, DM])
    wq2 = sbp("wq2", [128, DM])
    wq3 = sbp("wq3", [128, DM], BF16)

    cvt = sbp("cvt", [128, KT * 4])
    for k in range(KT):
        p.op("g", lambda g, k=k: g.dma_start(
            out=cvt[:, k * 4:(k + 1) * 4], in_=w_conv[k * 128:(k + 1) * 128, :]),
            reads=[w_conv], writes=[cvt], dma=True)
    for k in range(KT):
        for tap in range(4):
            col = tap * 128
            p.op("v", lambda v, k=k, tap=tap, col=col: v.scalar_tensor_tensor(
                wstage[:, col:col + 128], idn[:, :], 1.0,
                cvt[:, k * 4 + tap:k * 4 + tap + 1].broadcast_to([128, 128]),
                ALU.mult, ALU.mult),
                reads=[idn, cvt], writes=[wstage])
        p.op("q", lambda q, k=k: q.dma_start(
            out=scr_wcv.h[k], in_=wstage[:, 0:512]),
            reads=[wstage], writes=[scr_wcv], dma=True)


    wxstage = sbp("wxstage", [128, 128])
    for k in range(KT):
        p.op("g", lambda g, k=k: g.dma_start(
            out=wxstage[0:96, 0:128], in_=w_xproj[:, k * 128:(k + 1) * 128],
            allow_slow_non_contiguous=True),
            reads=[w_xproj], writes=[wxstage], dma=True)
        p.op("t", lambda t: t.transpose(ptF[:, 0:128], wxstage[:, 0:128], idf[:, :]),
             reads=[wxstage, idf], writes=[ptF])
        p.op("s", lambda s, k=k: s.copy(wxt[:, k * 96:(k + 1) * 96], ptF[:, 0:96]),
             reads=[ptF], writes=[wxt])
    for m in range(KT):
        p.op("g", lambda g, m=m: g.dma_start(
            out=wstage[:, 0:RNK], in_=w_dtproj[m * 128:(m + 1) * 128, :]),
            reads=[w_dtproj], writes=[wstage], dma=True)
        p.op("t", lambda t: t.transpose(ptB[:, 0:128], wstage[:, 0:128], idn[:, :]),
             reads=[wstage, idn], writes=[ptB])
        p.op("s", lambda s, m=m: s.copy(wdtp[0:64, m * 128:(m + 1) * 128], ptB[0:64, 0:128]),
             reads=[ptB], writes=[wdtp])

    def prep_dram_T(src, rows, cols, dst_scr, fold_dst=None, fold_beta=None,
                    col_scale=None):
        nk = cols // 128
        for m in range(rows // 128):
            p.op("g", lambda g, m=m: g.dma_start(
                out=wstage[:, 0:cols], in_=src[m * 128:(m + 1) * 128, :]),
                reads=[src], writes=[wstage], dma=True)
            for k in range(nk):
                p.op("t", lambda t, k=k: t.transpose(
                    ptB[:, 0:128], wstage[:, k * 128:(k + 1) * 128], idn[:, :]),
                    reads=[wstage, idn], writes=[ptB])
                p.op("s", lambda s, k=k: s.copy(
                    wstage[:, 4096 + k * 128:4096 + k * 128 + 128], ptB[:, 0:128]),
                    reads=[ptB], writes=[wstage])
            if fold_dst is not None:
                items = []
                for k in range(nk):
                    def em(t, k=k, first=(k == 0), last=(k == nk - 1)):
                        return t.matmul(pbB[:, 0:1],
                                        wstage[:, 4096 + k * 128:4096 + k * 128 + 128],
                                        fold_beta[:, k:k + 1], start=first, stop=last)
                    items.append((em, [wstage, fold_beta]))
                mm_group(p, pbB, items)
                p.op("s", lambda s, m=m: s.copy(fold_dst[:, m:m + 1], pbB[:, 0:1]),
                     reads=[pbB], writes=[fold_dst])
            if col_scale is not None:
                for k in range(nk):
                    p.op("v", lambda v, k=k: v.tensor_scalar(
                        wstage[:, 4096 + k * 128:4096 + k * 128 + 128],
                        wstage[:, 4096 + k * 128:4096 + k * 128 + 128],
                        col_scale[:, k:k + 1], None, ALU.mult),
                        reads=[wstage, col_scale], writes=[wstage])
            p.op("q", lambda q, m=m, nk=nk: q.dma_start(
                out=dst_scr.h[m], in_=wstage[:, 4096:4096 + nk * 128]),
                reads=[wstage], writes=[dst_scr], dma=True)

    prep_dram_T(w_inproj, 2 * DH, DM, scr_wiT, fold_dst=wb1, fold_beta=b1tt,
                col_scale=g1t)
    half8 = sbp("half8", [128, KT])
    p.op("v", lambda v: v.memset(half8[:, :], 0.5), writes=[half8])
    prep_dram_T(w_outproj, DM, DH, scr_woT, col_scale=half8)
    prep_dram_T(w_ffn1, DMID, DM, scr_w1T, fold_dst=cf1, fold_beta=b2tt,
                col_scale=g2t)
    p.op("v", lambda v: v.tensor_tensor(cf1[:, :], cf1[:, :], bf1t[:, :], ALU.add),
         reads=[cf1, bf1t], writes=[cf1])
    prep_dram_T(w_ffn2, DM, DMID, scr_w2T)

    def prep_quant(src, rows, cols, dst_scr, sc_tile):
        nk = cols // 128
        for m in range(rows // 128):
            p.op("g", lambda g, m=m: g.dma_start(
                out=wloadf[:, 0:cols], in_=src[m * 128:(m + 1) * 128, :]),
                reads=[src], writes=[wloadf], dma=True)
            p.op("v", lambda v, cols=cols: v.tensor_reduce(
                st1[:, 60:61], wloadf[:, 0:cols], AXX, ALU.max,
                apply_absolute_value=True), reads=[wloadf], writes=[st1])
            p.op("v", lambda v, m=m: v.tensor_scalar(
                sc_tile[:, m:m + 1], st1[:, 60:61], 1.0 / QMAX, None, ALU.mult),
                reads=[st1], writes=[sc_tile])
            p.op("v", lambda v: v.reciprocal(st1[:, 61:62], st1[:, 60:61]),
                 reads=[st1], writes=[st1])
            p.op("v", lambda v: v.tensor_scalar(
                st1[:, 61:62], st1[:, 61:62], QMAX, None, ALU.mult),
                reads=[st1], writes=[st1])
            for j0 in range(0, cols, DM):
                w = min(DM, cols - j0)
                p.op("v", lambda v, j0=j0, w=w: v.tensor_scalar(
                    wq1[:, 0:w], wloadf[:, j0:j0 + w], st1[:, 61:62], None, ALU.mult),
                    reads=[wloadf, st1], writes=[wq1])
                p.op("v", lambda v, w=w: v.tensor_copy(wqi[:, 0:w], wq1[:, 0:w]),
                     reads=[wq1], writes=[wqi])
                p.op("v", lambda v, w=w: v.tensor_copy(wq2[:, 0:w], wqi[:, 0:w]),
                     reads=[wqi], writes=[wq2])
                p.op("v", lambda v, w=w: v.tensor_tensor(
                    wq1[:, 0:w], wq1[:, 0:w], wq2[:, 0:w], ALU.subtract),
                    reads=[wq1, wq2], writes=[wq1])
                p.op("v", lambda v, w=w: v.scalar_tensor_tensor(
                    wq2[:, 0:w], wq1[:, 0:w], 0.5, wq2[:, 0:w], ALU.is_gt, ALU.add),
                    reads=[wq1, wq2], writes=[wq2])
                p.op("v", lambda v, w=w: v.scalar_tensor_tensor(
                    wq2[:, 0:w], wq1[:, 0:w], -0.5, wq2[:, 0:w], ALU.is_lt, ALU.subtract),
                    reads=[wq1, wq2], writes=[wq2])
                p.op("v", lambda v, w=w: v.tensor_scalar(
                    wq3[:, 0:w], wq2[:, 0:w], -QMAX - 1.0, QMAX, ALU.max, ALU.min),
                    reads=[wq2], writes=[wq3])
                for kk in range(w // 128):
                    k = (j0 // 128) + kk
                    p.op("t", lambda t, kk=kk: t.transpose(
                        ptB[:, 0:128], wq3[:, kk * 128:(kk + 1) * 128], idn[:, :]),
                        reads=[wq3, idn], writes=[ptB])
                    p.op("s", lambda s, k=k: s.copy(
                        wstage[:, 4096 + k * 128:4096 + k * 128 + 128], ptB[:, 0:128]),
                        reads=[ptB], writes=[wstage])
            p.op("q", lambda q, m=m, nk=nk: q.dma_start(
                out=dst_scr.h[m], in_=wstage[:, 4096:4096 + nk * 128]),
                reads=[wstage], writes=[dst_scr], dma=True)

    prep_quant(w_gate, DQ, DM, scr_wgT, scg)
    prep_quant(w_up, DQ, DM, scr_wuT, scu)
    prep_quant(w_down, DM, DQ, scr_wdT, scdt)

    p.barrier()
    es_pp.close()

    def ln_stats(src_tile, ntt, scol, junk_bf, junk_f32):
        for i in range(ntt):
            p.op("v", lambda v, i=i: v.tensor_scalar(
                junk_bf[:, 0:DM], src_tile[:, i * DM:(i + 1) * DM],
                1.0, 0.0, ALU.mult, ALU.add, accum_out=st1[:, scol + i:scol + i + 1]),
                reads=[src_tile], writes=[junk_bf, st1])
            p.op("s", lambda s, i=i: s.activation(
                junk_f32[:, 0:DM], src_tile[:, i * DM:(i + 1) * DM],
                AF.Square, accum_out=st1[:, scol + 4 + i:scol + 4 + i + 1]),
                reads=[src_tile], writes=[junk_f32, st1])
        p.op("v", lambda v: v.tensor_scalar(
            st1[:, scol:scol + ntt], st1[:, scol:scol + ntt], 1.0 / DM, None, ALU.mult),
            reads=[st1], writes=[st1])
        p.op("v", lambda v: v.tensor_scalar(
            st1[:, scol + 4:scol + 4 + ntt], st1[:, scol + 4:scol + 4 + ntt],
            1.0 / DM, None, ALU.mult), reads=[st1], writes=[st1])
        p.op("v", lambda v: v.scalar_tensor_tensor(
            st1[:, scol + 12:scol + 12 + ntt], st1[:, scol:scol + ntt], 1.0,
            st1[:, scol:scol + ntt], ALU.mult, ALU.mult),
            reads=[st1], writes=[st1])
        p.op("v", lambda v: v.tensor_tensor(
            st1[:, scol + 4:scol + 4 + ntt], st1[:, scol + 4:scol + 4 + ntt],
            st1[:, scol + 12:scol + 12 + ntt], ALU.subtract),
            reads=[st1], writes=[st1])
        p.op("s", lambda s: s.activation(
            st1[:, scol + 12:scol + 12 + ntt], st1[:, scol + 4:scol + 4 + ntt],
            AF.Ln, bias=cns[:, 0:1]), reads=[st1, cns], writes=[st1])
        p.op("s", lambda s: s.activation(
            st1[:, scol + 8:scol + 8 + ntt], st1[:, scol + 12:scol + 12 + ntt],
            AF.Exp, scale=cns[:, 1:2]), reads=[st1, cns], writes=[st1])
        p.op("v", lambda v: v.scalar_tensor_tensor(
            st1[:, scol + 16:scol + 16 + ntt], st1[:, scol:scol + ntt], -1.0,
            st1[:, scol + 8:scol + 8 + ntt], ALU.mult, ALU.mult),
            reads=[st1], writes=[st1])

    es_p1 = ExitStack()
    sb1 = mksb(es_p1)
    xt1 = sb1("xt1", [128, 2 * DM])
    ub1 = sb1("ub1", [128, DM], BF16)
    ufe = sb1("ufe", [128, MT * TC], BF16)
    xib = sb1("xib", [128, KT * (TC + 3)], BF16)
    szr = sb1("szr", [128, 2 * TC])
    xcf = sb1("xcf", [128, KT * TC])
    dtf = sb1("dtf", [128, KT * TC])
    brep = sb1("brep", [128, NST * TC])
    crep = sb1("crep", [128, NST * TC])
    dxc = sb1("dxc", [128, 2 * TC])
    dab = sb1("dab", [128, 2 * TC])
    dbxt = sb1("dbxt", [128, 4 * TC])
    hall = sb1("hall", [128, 4 * TC])
    qnb = sb1("qnb", [128, 4 * TC], BF16)
    hcar = sb1("hcar", [128, KT * NST])
    ygb = sb1("ygb", [128, KT * TC], BF16)
    otk = sb1("otk", [128, 4 * DM])
    wits = sb1("wits", [128, MT * 128], BF16)
    wzs = sb1("wzs", [128, MT * 128], BF16)
    wots = sb1("wots", [128, KT * 128], BF16)
    wcvs = sb1("wcvs", [128, 512], BF16)

    xt1S = mk_slots(xt1, 2)
    dabS = mk_slots(dab, 2)
    hallS = mk_slots(hall, 4)
    dxcS = mk_slots(dxc, 2)
    szrS = mk_slots(szr, 2)
    vt1S = mk_slots(vt1, 2)
    vt2S = mk_slots(vt2, 2)
    p.op("v", lambda v: v.memset(hcar[:, :], 0.0), writes=[hcar])
    p.op("v", lambda v: v.memset(xib[:, :], 0.0), writes=[xib])

    for c in range(NCH):
        for i in range(TT):
            sl = i % 2
            p.op("q", lambda q, c=c, i=i, sl=sl: q.dma_start(
                out=xt1[:, sl * DM:(sl + 1) * DM],
                in_=xfull[c * TC + i * 128: c * TC + (i + 1) * 128, :]),
                reads=[xfull], writes=[xt1S[sl]], dma=True)
            p.op("v", lambda v, i=i, sl=sl: v.tensor_scalar(
                ub1[:, 0:DM], xt1[:, sl * DM:(sl + 1) * DM],
                1.0, 0.0, ALU.mult, ALU.add, accum_out=st1[:, i:i + 1]),
                reads=[xt1S[sl]], writes=[ub1, st1])
            p.op("s", lambda s, i=i, sl=sl: s.activation(
                vt1[:, 0:512], xt1[:, sl * DM:sl * DM + 512],
                AF.Square, accum_out=st1[:, 8 + i:9 + i]),
                reads=[xt1S[sl]], writes=[vt1S[0], st1])
            p.op("s", lambda s, i=i, sl=sl: s.activation(
                vt2[:, 0:512], xt1[:, sl * DM + 512:sl * DM + 1024],
                AF.Square, accum_out=st1[:, 12 + i:13 + i]),
                reads=[xt1S[sl]], writes=[vt2S[0], st1])
        p.op("v", lambda v: v.tensor_tensor(
            st1[:, 8:8 + TT], st1[:, 8:8 + TT], st1[:, 12:12 + TT], ALU.add),
            reads=[st1], writes=[st1])
        p.op("v", lambda v: v.tensor_scalar(
            st1[:, 0:TT], st1[:, 0:TT], 1.0 / DM, None, ALU.mult),
            reads=[st1], writes=[st1])
        p.op("v", lambda v: v.tensor_scalar(
            st1[:, 8:8 + TT], st1[:, 8:8 + TT], 1.0 / DM, None, ALU.mult),
            reads=[st1], writes=[st1])
        p.op("v", lambda v: v.scalar_tensor_tensor(
            st1[:, 16:16 + TT], st1[:, 0:TT], 1.0, st1[:, 0:TT], ALU.mult, ALU.mult),
            reads=[st1], writes=[st1])
        p.op("v", lambda v: v.tensor_tensor(
            st1[:, 8:8 + TT], st1[:, 8:8 + TT], st1[:, 16:16 + TT], ALU.subtract),
            reads=[st1], writes=[st1])
        p.op("s", lambda s: s.activation(
            st1[:, 16:16 + TT], st1[:, 8:8 + TT], AF.Ln, bias=cns[:, 0:1]),
            reads=[st1, cns], writes=[st1])
        p.op("s", lambda s: s.activation(
            st1[:, 20:20 + TT], st1[:, 16:16 + TT], AF.Exp, scale=cns[:, 1:2]),
            reads=[st1, cns], writes=[st1])
        p.op("v", lambda v: v.scalar_tensor_tensor(
            st1[:, 24:24 + TT], st1[:, 8:8 + TT], 1.0, st1[:, 20:20 + TT],
            ALU.mult, ALU.mult), reads=[st1], writes=[st1])
        p.op("v", lambda v: v.tensor_tensor(
            st1[:, 24:24 + TT], st1[:, 24:24 + TT], st1[:, 20:20 + TT], ALU.mult),
            reads=[st1], writes=[st1])
        p.op("s", lambda s: s.activation(
            st1[:, 28:28 + TT], st1[:, 24:24 + TT], AF.Ln, bias=cns[:, 0:1]),
            reads=[st1, cns], writes=[st1])
        p.op("s", lambda s: s.activation(
            st1[:, 28:28 + TT], st1[:, 28:28 + TT], AF.Exp, scale=cns[:, 1:2]),
            reads=[st1, cns], writes=[st1])
        p.op("v", lambda v: v.tensor_tensor(
            st1[:, 32:32 + TT], st1[:, 20:20 + TT], st1[:, 28:28 + TT], ALU.mult),
            reads=[st1], writes=[st1])
        p.op("v", lambda v: v.scalar_tensor_tensor(
            st1[:, 36:36 + TT], st1[:, 0:TT], -1.0, st1[:, 32:32 + TT],
            ALU.mult, ALU.mult), reads=[st1], writes=[st1])
        for i in range(TT):
            sl = i % 2
            p.op("q", lambda q, c=c, i=i, sl=sl: q.dma_start(
                out=xt1[:, sl * DM:(sl + 1) * DM],
                in_=xfull[c * TC + i * 128: c * TC + (i + 1) * 128, :]),
                reads=[xfull], writes=[xt1S[sl]], dma=True)
            p.op("s", lambda s, i=i, sl=sl: s.activation(
                ub1[:, 0:DM], xt1[:, sl * DM:(sl + 1) * DM], AF.Identity,
                scale=st1[:, 32 + i:33 + i], bias=st1[:, 36 + i:37 + i]),
                reads=[xt1S[sl], st1], writes=[ub1])
            for k in range(MT):
                p.op("t", lambda t, k=k: t.transpose(
                    ptB[:, k * 128:(k + 1) * 128], ub1[:, k * 128:(k + 1) * 128],
                    idn[:, :]), reads=[ub1, idn], writes=[ptB])
            p.op("s", lambda s, i=i: s.copy(
                bass.AP(ufe.h, i * 128, [[MT * TC, 128], [TC, MT], [1, 128]]),
                ptB[:, 0:MT * 128]),
                reads=[ptB], writes=[ufe])
        for m in range(KT):
            wsl = m % 2
            p.op("q", lambda q, m=m, wsl=wsl: q.dma_start(
                out=wits[:, 0:1024], in_=scr_wiT.h[m]),
                reads=[scr_wiT], writes=[wits], dma=True)
            pb = pbA if m % 2 == 0 else pbB
            items = []
            for k in range(MT):
                def em(t, m=m, k=k, pb=pb, wsl=wsl, first=(k == 0), last=(k == MT - 1)):
                    return t.matmul(pb[:, 0:TC],
                                    wits[:, k * 128:k * 128 + 128],
                                    ufe[:, k * TC:(k + 1) * TC], start=first, stop=last)
                items.append((em, [wits, ufe]))
            mm_group(p, pb, items)
            p.op("s", lambda s, m=m: s.copy(
                xib[:, m * (TC + 3):m * (TC + 3) + 3],
                xib[:, m * (TC + 3) + TC:m * (TC + 3) + TC + 3]),
                reads=[xib], writes=[xib])
            p.op("s", lambda s, m=m, pb=pb: s.activation(
                xib[:, m * (TC + 3) + 3:m * (TC + 3) + 3 + TC], pb[:, 0:TC],
                AF.Identity, bias=wb1[:, m:m + 1]),
                reads=[pb, wb1], writes=[xib])
        for k in range(KT):
            wsl = k % 2
            p.op("q", lambda q, k=k, wsl=wsl: q.dma_start(
                out=wcvs[:, 0:512], in_=scr_wcv.h[k]),
                reads=[scr_wcv], writes=[wcvs], dma=True)
            pb = pbC if k % 2 == 0 else pbD
            items = []
            for tap in range(4):
                def em(t, k=k, tap=tap, pb=pb, wsl=wsl, first=(tap == 0), last=(tap == 3)):
                    return t.matmul(pb[:, 0:TC],
                                    wcvs[:, tap * 128:tap * 128 + 128],
                                    xib[:, k * (TC + 3) + tap:k * (TC + 3) + tap + TC],
                                    start=first, stop=last)
                items.append((em, [wcvs, xib]))
            mm_group(p, pb, items)
            p.op("s", lambda s, k=k, pb=pb: s.activation(
                xcf[:, k * TC:(k + 1) * TC], pb[:, 0:TC], AF.Silu, bias=cbs[:, k:k + 1]),
                reads=[pb, cbs], writes=[xcf])
        items = []
        for k in range(KT):
            def em(t, k=k, first=(k == 0), last=(k == KT - 1)):
                return t.matmul(pbX[:, 0:TC], wxt[:, k * 96:(k + 1) * 96],
                                xcf[:, k * TC:(k + 1) * TC], start=first, stop=last)
            items.append((em, [wxt, xcf]))
        mm_group(p, pbX, items)
        p.op("s", lambda s: s.copy(vt1[0:96, 0:TC], pbX[0:96, 0:TC]),
             reads=[pbX], writes=[vt1S[0]])
        p.op("q", lambda q, c=c: q.dma_start(
            out=scr_xdin.h[c], in_=vt1[0:96, 0:TC]),
            reads=[vt1S[0]], writes=[scr_xdin], dma=True)
        p.op("g", lambda g, c=c: g.collective_compute(
            "AllReduce", ALU.add, replica_groups=PAIRS,
            ins=[scr_xdin.h[c].opt()], outs=[scr_xdout.h[c].opt()]),
            reads=[scr_xdin], writes=[scr_xdout], coll=True)
        p.op("q", lambda q, c=c: q.dma_start(
            out=xdbt[:, 0:TC], in_=scr_xdout.h[c]),
            reads=[scr_xdout], writes=[xdbt], dma=True)
        p.op("v", lambda v: v.tensor_copy(xdbb[0:64, 0:TC], xdbt[0:64, 0:TC]),
             reads=[xdbt], writes=[xdbb])
        for m in range(KT):
            pb = pbC if m % 2 == 0 else pbD
            sl = m % 2
            v1s, v2s = vt1S[sl], vt2S[sl]
            p.op("t", lambda t, m=m, pb=pb: t.matmul(
                pb[:, 0:TC], wdtp[0:64, m * 128:(m + 1) * 128], xdbb[0:64, 0:TC],
                start=True, stop=True), reads=[wdtp, xdbb], writes=[pb])
            p.op("s", lambda s, m=m, pb=pb, o=sl * 512: s.activation(
                vt1[:, o:o + TC], pb[:, 0:TC], AF.Identity, bias=dtbv[:, m:m + 1]),
                reads=[pb, dtbv], writes=[v1s])
            p.op("s", lambda s, m=m, pb=pb, o=sl * 512: s.activation(
                vt2[:, o:o + TC], pb[:, 0:TC], AF.Abs, bias=dtbv[:, m:m + 1]),
                reads=[pb, dtbv], writes=[v2s])
            p.op("s", lambda s, o=sl * 512: s.activation(
                vt2[:, o:o + TC], vt2[:, o:o + TC], AF.Exp, scale=-1.0),
                reads=[v2s], writes=[v2s])
            p.op("s", lambda s, o=sl * 512: s.activation(
                vt2[:, o:o + TC], vt2[:, o:o + TC], AF.Ln, bias=1.0),
                reads=[v2s], writes=[v2s])
            p.op("v", lambda v, m=m, o=sl * 512: v.scalar_tensor_tensor(
                dtf[:, m * TC:(m + 1) * TC], vt1[:, o:o + TC], 0.0, vt2[:, o:o + TC],
                ALU.max, ALU.add), reads=[v1s, v2s], writes=[dtf])
        p.op("q", lambda q: q.dma_start(
            out=brep[0:1, :], in_=xdbt[64:80, 0:TC],
            allow_slow_non_contiguous=True),
            reads=[xdbt], writes=[brep], dma=True)
        p.op("q", lambda q: q.dma_start(
            out=crep[0:1, :], in_=xdbt[80:96, 0:TC],
            allow_slow_non_contiguous=True),
            reads=[xdbt], writes=[crep], dma=True)
        nrep = 1
        while nrep < 128:
            p.op("q", lambda q, n=nrep: q.dma_start(
                out=brep[n:2 * n, :], in_=brep[0:n, :]),
                reads=[brep], writes=[brep], dma=True)
            p.op("q", lambda q, n=nrep: q.dma_start(
                out=crep[n:2 * n, :], in_=crep[0:n, :]),
                reads=[crep], writes=[crep], dma=True)
            nrep *= 2
        for k in range(KT):
            p.op("v", lambda v, k=k: v.tensor_tensor(
                dxc[:, (k % 2) * TC:(k % 2 + 1) * TC], dtf[:, k * TC:(k + 1) * TC],
                xcf[:, k * TC:(k + 1) * TC], ALU.mult),
                reads=[dtf, xcf], writes=[dxcS[k % 2]])
            yitems = []
            for ng in range(4):
                p.op("g", lambda g, k=k, ng=ng: g.tensor_tensor(
                    dbxt[:, 0:4 * TC],
                    bass.AP(dxc.h, (k % 2) * TC, [[2 * TC, 128], [0, 4], [1, TC]]),
                    brep[:, ng * 4 * TC:(ng + 1) * 4 * TC], ALU.mult),
                    reads=[dxcS[k % 2], brep], writes=[dbxt])
                for n4 in range(4):
                    n = ng * 4 + n4
                    p.op("s", lambda s, k=k, n=n, sl2=n4 % 2: s.activation(
                        dab[:, sl2 * TC:(sl2 + 1) * TC], dtf[:, k * TC:(k + 1) * TC],
                        AF.Exp, scale=asbn[:, k * NST + n:k * NST + n + 1]),
                        reads=[dtf, asbn], writes=[dabS[n4 % 2]])
                    if c == 0:
                        ini = 0.0
                        rds = [dabS[n4 % 2], dbxt]
                    else:
                        ini = hcar[:, k * NST + n:k * NST + n + 1]
                        rds = [dabS[n4 % 2], dbxt, hcar]
                    p.op("v", lambda v, n4=n4, sl2=n4 % 2, ini=ini: v.tensor_tensor_scan(
                        hall[:, n4 * TC:(n4 + 1) * TC], dab[:, sl2 * TC:(sl2 + 1) * TC],
                        dbxt[:, n4 * TC:(n4 + 1) * TC],
                        ini, ALU.mult, ALU.add),
                        reads=rds, writes=[hallS[n4]])
                p.op("v", lambda v, ng=ng: v.tensor_tensor(
                    qnb[:, :], hall[:, :], crep[:, ng * 4 * TC:(ng + 1) * 4 * TC],
                    ALU.mult), reads=[hallS[0], hallS[1], hallS[2], hallS[3], crep],
                    writes=[qnb])
                for n4 in range(4):
                    n = ng * 4 + n4
                    def em(t, n=n, n4=n4, first=(n == 0), last=(n == NST - 1)):
                        return t.matmul(pbY[:, 0:TC], idn[:, :],
                                        qnb[:, n4 * TC:(n4 + 1) * TC],
                                        start=first, stop=last)
                    yitems.append((em, [idn, qnb]))
                p.op("q", lambda q, k=k, ng=ng: q.dma_start(
                    out=hcar[:, k * NST + ng * 4:k * NST + ng * 4 + 4],
                    in_=bass.AP(hall.h, TC - 1, [[4 * TC, 128], [TC, 4]]),
                    allow_slow_non_contiguous=True),
                    reads=[hallS[0], hallS[1], hallS[2], hallS[3]], writes=[hcar],
                    dma=True)
            mm_group(p, pbY, yitems)
            wsl = k % 2
            p.op("q", lambda q, k=k, wsl=wsl: q.dma_start(
                out=wzs[:, 0:1024], in_=scr_wiT.h[8 + k]),
                reads=[scr_wiT], writes=[wzs], dma=True)
            pbz = pbC if k % 2 == 0 else pbD
            items = []
            for kk in range(MT):
                def em(t, k=k, kk=kk, pbz=pbz, wsl=wsl, first=(kk == 0), last=(kk == MT - 1)):
                    return t.matmul(pbz[:, 0:TC],
                                    wzs[:, kk * 128:kk * 128 + 128],
                                    ufe[:, kk * TC:(kk + 1) * TC], start=first, stop=last)
                items.append((em, [wzs, ufe]))
            mm_group(p, pbz, items)
            sl = k % 2
            p.op("s", lambda s, k=k, pbz=pbz, o=sl * TC: s.activation(
                szr[:, o:o + TC], pbz[:, 0:TC],
                AF.Identity, bias=wb1[:, 8 + k:9 + k]),
                reads=[pbz, wb1], writes=[szrS[sl]])
            p.op("s", lambda s, k=k, pbz=pbz, o=sl * 512: s.activation(
                vt2[:, o:o + TC], szr[:, (k % 2) * TC:(k % 2 + 1) * TC],
                AF.Tanh, scale=cns[:, 3:4]),
                reads=[szrS[sl], cns], writes=[vt2S[sl]])
            p.op("v", lambda v, k=k, o=sl * 512, o2=sl * TC: v.scalar_tensor_tensor(
                szr[:, o2:o2 + TC], vt2[:, o:o + TC], 1.0, szr[:, o2:o2 + TC],
                ALU.add, ALU.mult), reads=[vt2S[sl], szrS[sl]], writes=[szrS[sl]])
            p.op("v", lambda v, k=k: v.scalar_tensor_tensor(
                gt1[:, 0:TC], xcf[:, k * TC:(k + 1) * TC], dsb[:, k:k + 1],
                pbY[:, 0:TC], ALU.mult, ALU.add), reads=[xcf, dsb, pbY], writes=[gt1])
            p.op("v", lambda v, k=k, o2=sl * TC: v.tensor_tensor(
                ygb[:, k * TC:(k + 1) * TC], gt1[:, 0:TC],
                szr[:, o2:o2 + TC],
                ALU.mult), reads=[gt1, szrS[sl]], writes=[ygb])
        for m in range(MT):
            wsl = m % 2
            p.op("q", lambda q, m=m, wsl=wsl: q.dma_start(
                out=wots[:, 0:1024],
                in_=scr_woT.h[m]), reads=[scr_woT], writes=[wots], dma=True)
            pb = pbA if m % 2 == 0 else pbB
            items = []
            for k in range(KT):
                def em(t, m=m, k=k, pb=pb, wsl=wsl, first=(k == 0), last=(k == KT - 1)):
                    return t.matmul(pb[:, 0:TC],
                                    wots[:, k * 128:k * 128 + 128],
                                    ygb[:, k * TC:(k + 1) * TC], start=first, stop=last)
                items.append((em, [wots, ygb]))
            mm_group(p, pb, items)
            p.op("s", lambda s, m=m, pb=pb: s.copy(
                gt1[:, 0:TC], pb[:, 0:TC]), reads=[pb], writes=[gt1])
            for i in range(TT):
                p.op("t", lambda t, m=m, i=i: t.transpose(
                    ptF[:, i * 128:(i + 1) * 128], gt1[:, i * 128:i * 128 + 128],
                    idf[:, :]), reads=[gt1, idf], writes=[ptF])
            p.op("s", lambda s, m=m: s.copy(
                bass.AP(otk.h, m * 128, [[4 * DM, 128], [DM, TT], [1, 128]]),
                ptF[:, 0:TT * 128]),
                reads=[ptF], writes=[otk])
        for i in range(TT):
            p.op("q", lambda q, c=c, i=i: q.dma_start(
                out=scr_rsin.h[c, i * 128:(i + 1) * 128, :],
                in_=otk[:, i * DM:(i + 1) * DM]),
                reads=[otk], writes=[scr_rsin], dma=True)
        p.op("g", lambda g, c=c: g.collective_compute(
            "ReduceScatter", ALU.add, replica_groups=PAIRS,
            ins=[scr_rsin.h[c].opt()], outs=[scr_rsout.h[c].opt()]),
            reads=[scr_rsin], writes=[scr_rsout], coll=True)

    p.barrier()
    es_p1.close()
    es_p2 = ExitStack()
    sb2 = mksb(es_p2)
    p2x = sb2("p2x", [128, 4 * DM])
    p2u = sb2("p2u", [128, 2 * DM])
    p2b = sb2("p2b", [128, DM], BF16)
    p2s = sb2("p2s", [128, MT * 128], BF16)
    p2j = sb2("p2j", [128, DM])
    ntt2 = MYC // 128
    for c in range(NCH):
        for i in range(ntt2):
            p.op("q", lambda q, c=c, i=i: q.dma_start(
                out=p2x[:, i * DM:(i + 1) * DM],
                in_=xmine[c * MYC + i * 128:c * MYC + (i + 1) * 128, :]),
                reads=[xmine], writes=[p2x], dma=True)
        ln_stats(p2x, ntt2, 0, p2b, p2j)
        for i in range(ntt2):
            p.op("s", lambda s, i=i: s.activation(
                p2u[:, i * DM:(i + 1) * DM], p2x[:, i * DM:(i + 1) * DM], AF.Identity,
                scale=st1[:, 8 + i:9 + i], bias=st1[:, 16 + i:17 + i]),
                reads=[p2x, st1], writes=[p2u])
        for i in range(ntt2):
            p.op("q", lambda q, c=c, i=i, o=ntt2 + i: q.dma_start(
                out=p2x[:, o * DM:(o + 1) * DM],
                in_=scr_rsout.h[c, i * 128:(i + 1) * 128, :]),
                reads=[scr_rsout], writes=[p2x], dma=True)
        for i in range(ntt2):
            p.op("v", lambda v, i=i, o=ntt2 + i: v.tensor_tensor(
                p2u[:, i * DM:(i + 1) * DM], p2u[:, i * DM:(i + 1) * DM],
                p2x[:, o * DM:(o + 1) * DM], ALU.add),
                reads=[p2u, p2x], writes=[p2u])
        for i in range(ntt2):
            p.op("q", lambda q, c=c, i=i: q.dma_start(
                out=scr_u1.h[c * MYC + i * 128:c * MYC + (i + 1) * 128, :],
                in_=p2u[:, i * DM:(i + 1) * DM]), reads=[p2u], writes=[scr_u1], dma=True)
        ln_stats(p2u, ntt2, 0, p2b, p2j)
        for i in range(ntt2):
            p.op("s", lambda s, i=i: s.activation(
                p2b[:, 0:DM], p2u[:, i * DM:(i + 1) * DM], AF.Identity,
                scale=st1[:, 8 + i:9 + i], bias=st1[:, 16 + i:17 + i]),
                reads=[p2u, st1], writes=[p2b])
            for k in range(MT):
                p.op("t", lambda t, k=k: t.transpose(
                    ptB[:, k * 128:(k + 1) * 128], p2b[:, k * 128:(k + 1) * 128],
                    idn[:, :]), reads=[p2b, idn], writes=[ptB])
            p.op("s", lambda s: s.copy(p2s[:, :], ptB[:, 0:MT * 128]),
                 reads=[ptB], writes=[p2s])
            p.op("q", lambda q, c=c, i=i: q.dma_start(
                out=bass.AP(scr_u2f.h.tensor,
                            scr_u2f.h.offset + c * MYC + i * 128,
                            [[MYT, 128], [128 * MYT, MT], [1, 128]]),
                in_=p2s[:, :]), reads=[p2s], writes=[scr_u2f], dma=True)

    p.barrier()
    es_p2.close()

    es_p3 = ExitStack()
    sb3 = mksb(es_p3)
    fgrp = sb3("fgrp", [128, KMID * GRP], BF16)
    u2g = sb3("u2g", [128, MT * GRP], BF16)
    w1s = sb3("w1s", [128, MT * 128], BF16)
    w2s = sb3("w2s", [128, KMID * 128], BF16)
    ofe3 = sb3("ofe3", [128, MT * GRP])
    otk3 = sb3("otk3", [128, 4 * DM])
    u1t3 = sb3("u1t3", [128, 2 * DM])
    junkb = sb3("junkb", [128, DM], BF16)
    junkf = sb3("junkf", [128, DM])

    p.op("v", lambda v: v.memset(gmaxt[:, 0:1], 0.0), writes=[gmaxt])
    p.op("v", lambda v: v.memset(gmaxt[:, 2:3], 0.0), writes=[gmaxt])
    for g_ in range(NGRP):
        for k in range(MT):
            p.op("q", lambda q, g_=g_, k=k: q.dma_start(
                out=u2g[:, k * GRP:(k + 1) * GRP],
                in_=scr_u2f.h[k, :, g_ * GRP:(g_ + 1) * GRP]),
                reads=[scr_u2f], writes=[u2g], dma=True)
        for m in range(KMID):
            p.op("q", lambda q, m=m: q.dma_start(
                out=w1s[:, :], in_=scr_w1T.h[m]), reads=[scr_w1T], writes=[w1s], dma=True)
            for j in range(JPG):
                pb = pbA if (m + j) % 2 == 0 else pbB
                items = []
                for k in range(MT):
                    def em(t, m=m, j=j, k=k, pb=pb, first=(k == 0), last=(k == MT - 1)):
                        return t.matmul(pb[:, 0:JW],
                                        w1s[:, k * 128:(k + 1) * 128],
                                        u2g[:, k * GRP + j * JW:k * GRP + j * JW + JW],
                                        start=first, stop=last)
                    items.append((em, [w1s, u2g]))
                mm_group(p, pb, items)
                p.op("s", lambda s, m=m, j=j, pb=pb: s.activation(
                    fgrp[:, m * GRP + j * JW:m * GRP + j * JW + JW], pb[:, 0:JW],
                    AF.Relu, bias=cf1[:, m:m + 1]), reads=[pb, cf1], writes=[fgrp])
        for m in range(MT):
            p.op("q", lambda q, m=m: q.dma_start(
                out=w2s[:, :], in_=scr_w2T.h[m]), reads=[scr_w2T], writes=[w2s], dma=True)
            for j in range(JPG):
                pb = pbC if (m + j) % 2 == 0 else pbD
                items = []
                for k in range(KMID):
                    def em(t, m=m, j=j, k=k, pb=pb, first=(k == 0), last=(k == KMID - 1)):
                        return t.matmul(pb[:, 0:JW],
                                        w2s[:, k * 128:(k + 1) * 128],
                                        fgrp[:, k * GRP + j * JW:k * GRP + j * JW + JW],
                                        start=first, stop=last)
                    items.append((em, [w2s, fgrp]))
                mm_group(p, pb, items)
                p.op("s", lambda s, m=m, j=j, pb=pb: s.activation(
                    ofe3[:, m * GRP + j * JW:m * GRP + j * JW + JW], pb[:, 0:JW],
                    AF.Identity, bias=bf2t[:, m:m + 1]), reads=[pb, bf2t], writes=[ofe3])
        for j in range(JPG):
            for i in range(JI):
                for m in range(MT):
                    p.op("t", lambda t, m=m, j=j, i=i: t.transpose(
                        ptF[:, 0:128],
                        ofe3[:, m * GRP + j * JW + i * 128:m * GRP + j * JW + i * 128 + 128],
                        idf[:, :]), reads=[ofe3, idf], writes=[ptF])
                    p.op("s", lambda s, m=m, i=i: s.copy(
                        otk3[:, i * DM + m * 128:i * DM + m * 128 + 128], ptF[:, 0:128]),
                        reads=[ptF], writes=[otk3])
            for i in range(JI):
                row0 = g_ * GRP + j * JW + i * 128
                p.op("q", lambda q, row0=row0: q.dma_start(
                    out=u1t3[:, 0:DM], in_=scr_u1.h[row0:row0 + 128, :]),
                    reads=[scr_u1], writes=[u1t3], dma=True)
                p.op("q", lambda q, row0=row0: q.dma_start(
                    out=u1t3[:, DM:2 * DM], in_=xmine[row0:row0 + 128, :]),
                    reads=[xmine], writes=[u1t3], dma=True)
                p.op("v", lambda v, i=i: v.tensor_tensor(
                    otk3[:, i * DM:(i + 1) * DM], otk3[:, i * DM:(i + 1) * DM],
                    u1t3[:, 0:DM], ALU.add), reads=[otk3, u1t3], writes=[otk3])
                p.op("v", lambda v, i=i: v.tensor_tensor(
                    otk3[:, i * DM:(i + 1) * DM], otk3[:, i * DM:(i + 1) * DM],
                    u1t3[:, DM:2 * DM], ALU.add), reads=[otk3, u1t3], writes=[otk3])
                p.op("q", lambda q, row0=row0, i=i: q.dma_start(
                    out=scr_xp.h[row0:row0 + 128, :], in_=otk3[:, i * DM:(i + 1) * DM]),
                    reads=[otk3], writes=[scr_xp], dma=True)
            ln_stats(otk3, JI, 32, junkb, junkf)
            for i in range(JI):
                trow = (g_ * GRP + j * JW) // 128 + i
                p.op("v", lambda v, trow=trow, i=i: v.tensor_copy(
                    lnos[:, trow:trow + 1], st1[:, 40 + i:41 + i]),
                    reads=[st1], writes=[lnos])
                p.op("v", lambda v, trow=trow, i=i: v.tensor_copy(
                    lnos[:, NTT + trow:NTT + trow + 1], st1[:, 48 + i:49 + i]),
                    reads=[st1], writes=[lnos])
                p.op("s", lambda s, i=i: s.activation(
                    junkf[:, 0:DM], otk3[:, i * DM:(i + 1) * DM], AF.Identity,
                    scale=st1[:, 40 + i:41 + i], bias=st1[:, 48 + i:49 + i]),
                    reads=[otk3, st1], writes=[junkf])
                p.op("v", lambda v: v.tensor_reduce(
                    gmaxt[:, 1:2], junkf[:, 0:DM], AXX, ALU.max,
                    apply_absolute_value=True), reads=[junkf], writes=[gmaxt])
                p.op("v", lambda v: v.tensor_tensor(
                    gmaxt[:, 0:1], gmaxt[:, 0:1], gmaxt[:, 1:2], ALU.max),
                    reads=[gmaxt], writes=[gmaxt])

    def global_max_and_bcast(local_col, bounce_in, bounce_out, dst_col):
        p.op("t", lambda t: t.transpose(ptF[0:1, 0:128], gmaxt[:, local_col:local_col + 1],
                                        idf[:, :]), reads=[gmaxt, idf], writes=[ptF])
        p.op("s", lambda s: s.copy(vt1[0:1, 0:128], ptF[0:1, 0:128]),
             reads=[ptF], writes=[vt1])
        p.op("v", lambda v: v.tensor_reduce(
            vt1[0:1, 128:129], vt1[0:1, 0:128], AXX, ALU.max),
            reads=[vt1], writes=[vt1])
        p.op("q", lambda q: q.dma_start(out=gmaxt[0:1, 3:4], in_=vt1[0:1, 128:129],
                                        allow_slow_non_contiguous=True),
             reads=[vt1], writes=[gmaxt], dma=True)
        nrep = 1
        while nrep < 128:
            p.op("q", lambda q, n=nrep: q.dma_start(
                out=gmaxt[n:2 * n, 3:4], in_=gmaxt[0:n, 3:4],
                allow_slow_non_contiguous=True),
                reads=[gmaxt], writes=[gmaxt], dma=True)
            nrep *= 2
        p.op("q", lambda q: q.dma_start(out=bounce_in.h[:, :], in_=gmaxt[:, 3:4],
                                        allow_slow_non_contiguous=True),
             reads=[gmaxt], writes=[bounce_in], dma=True)
        p.op("g", lambda g: g.collective_compute(
            "AllReduce", ALU.max, replica_groups=ALL8,
            ins=[bounce_in.h[:, :].opt()], outs=[bounce_out.h[:, :].opt()]),
            reads=[bounce_in], writes=[bounce_out], coll=True)
        p.op("q", lambda q, dst_col=dst_col: q.dma_start(
            out=qsc[:, dst_col:dst_col + 1], in_=bounce_out.h[:, :],
            allow_slow_non_contiguous=True),
            reads=[bounce_out], writes=[qsc], dma=True)

    global_max_and_bcast(0, scr_mx, scr_mxo, 0)
    p.op("g", lambda g: g.dma_start(
        out=clt[0:1, 0:1], in_=clip_x[0:1], allow_slow_non_contiguous=True),
        reads=[clip_x], writes=[clt], dma=True)
    p.op("g", lambda g: g.dma_start(
        out=clt[0:1, 1:2], in_=clip_down[0:1], allow_slow_non_contiguous=True),
        reads=[clip_down], writes=[clt], dma=True)
    nrep = 1
    while nrep < 128:
        p.op("q", lambda q, n=nrep: q.dma_start(
            out=clt[n:2 * n, 0:2], in_=clt[0:n, 0:2],
            allow_slow_non_contiguous=True),
            reads=[clt], writes=[clt], dma=True)
        nrep *= 2
    p.op("s", lambda s: s.activation(qsc[:, 2:4], clt[:, 0:2], AF.Exp, scale=-1.0),
         reads=[clt], writes=[qsc])
    p.op("v", lambda v: v.tensor_scalar(qsc[:, 2:4], qsc[:, 2:4], 1.0, None, ALU.add),
         reads=[qsc], writes=[qsc])
    p.op("v", lambda v: v.reciprocal(qsc[:, 2:4], qsc[:, 2:4]),
         reads=[qsc], writes=[qsc])
    p.op("v", lambda v: v.scalar_tensor_tensor(
        qsc[:, 0:1], qsc[:, 0:1], 1.0 / QMAX, qsc[:, 2:3], ALU.mult, ALU.mult),
        reads=[qsc], writes=[qsc])
    p.op("v", lambda v: v.reciprocal(qsc[:, 4:5], qsc[:, 0:1]),
         reads=[qsc], writes=[qsc])
    p.op("v", lambda v: v.tensor_scalar(
        scg[:, :], scg[:, :], qsc[:, 0:1], None, ALU.mult), reads=[scg, qsc], writes=[scg])
    p.op("v", lambda v: v.tensor_scalar(
        scu[:, :], scu[:, :], qsc[:, 0:1], None, ALU.mult), reads=[scu, qsc], writes=[scu])

    xqf = sb3("xqf", [128, MT * GRP], BF16)
    wgs = sb3("wgs", [128, MT * 128], BF16)
    wus = sb3("wus", [128, MT * 128], BF16)
    hg1 = sb3("hg1", [128, 512])
    hg2 = sb3("hg2", [128, 512])
    qv1 = sb3("qv1", [128, DM])
    qv2 = sb3("qv2", [128, DM])
    qvi = sb3("qvi", [128, DM], I32)
    qvb = sb3("qvb", [128, DM], BF16)

    def quant_chain(width, src_emit, src_reads, dst_tile, dst_slice, inv_col,
                    src_stream="s", src_dma=False):
        p.op(src_stream, src_emit, reads=src_reads, writes=[qv1], dma=src_dma)
        p.op("v", lambda v: v.tensor_scalar(
            qv1[:, 0:width], qv1[:, 0:width], qsc[:, inv_col:inv_col + 1], 1e6,
            ALU.mult, ALU.min), reads=[qv1, qsc], writes=[qv1])
        p.op("v", lambda v: v.tensor_scalar(
            qv1[:, 0:width], qv1[:, 0:width], -1e6, None, ALU.max),
            reads=[qv1], writes=[qv1])
        p.op("v", lambda v: v.tensor_copy(qvi[:, 0:width], qv1[:, 0:width]),
             reads=[qv1], writes=[qvi])
        p.op("v", lambda v: v.tensor_copy(qv2[:, 0:width], qvi[:, 0:width]),
             reads=[qvi], writes=[qv2])
        p.op("v", lambda v: v.tensor_tensor(
            qv1[:, 0:width], qv1[:, 0:width], qv2[:, 0:width], ALU.subtract),
            reads=[qv1, qv2], writes=[qv1])
        p.op("v", lambda v: v.scalar_tensor_tensor(
            qv2[:, 0:width], qv1[:, 0:width], 0.5, qv2[:, 0:width], ALU.is_gt, ALU.add),
            reads=[qv1, qv2], writes=[qv2])
        p.op("v", lambda v: v.scalar_tensor_tensor(
            qv2[:, 0:width], qv1[:, 0:width], -0.5, qv2[:, 0:width], ALU.is_lt,
            ALU.subtract), reads=[qv1, qv2], writes=[qv2])
        p.op("v", lambda v: v.tensor_scalar(
            dst_tile[:, dst_slice[0]:dst_slice[1]], qv2[:, 0:width],
            -QMAX - 1.0, QMAX, ALU.max, ALU.min),
            reads=[qv2], writes=[dst_tile])

    for g_ in range(NGRP):
        for i in range(GRP // 128):
            row0 = g_ * GRP + i * 128
            trow = row0 // 128
            p.op("q", lambda q, row0=row0: q.dma_start(
                out=u1t3[:, 0:DM], in_=scr_xp.h[row0:row0 + 128, :]),
                reads=[scr_xp], writes=[u1t3], dma=True)
            quant_chain(
                DM,
                lambda s, trow=trow: s.activation(
                    qv1[:, 0:DM], u1t3[:, 0:DM], AF.Identity,
                    scale=lnos[:, trow:trow + 1],
                    bias=lnos[:, NTT + trow:NTT + trow + 1]),
                [u1t3, lnos], qvb, (0, DM), 4)
            for k in range(MT):
                p.op("t", lambda t, k=k: t.transpose(
                    ptB[:, 0:128], qvb[:, k * 128:(k + 1) * 128], idn[:, :]),
                    reads=[qvb, idn], writes=[ptB])
                p.op("s", lambda s, k=k, i=i: s.copy(
                    xqf[:, k * GRP + i * 128:k * GRP + i * 128 + 128], ptB[:, 0:128]),
                    reads=[ptB], writes=[xqf])
        for m in range(KQ):
            p.op("q", lambda q, m=m: q.dma_start(
                out=wgs[:, :], in_=scr_wgT.h[m]), reads=[scr_wgT], writes=[wgs], dma=True)
            p.op("q", lambda q, m=m: q.dma_start(
                out=wus[:, :], in_=scr_wuT.h[m]), reads=[scr_wuT], writes=[wus], dma=True)
            for j in range(JPG):
                pb = pbA if j % 2 == 0 else pbB
                items = []
                for k in range(MT):
                    def em(t, m=m, j=j, k=k, pb=pb, first=(k == 0), last=(k == MT - 1)):
                        return t.matmul(pb[:, 0:JW],
                                        wgs[:, k * 128:(k + 1) * 128],
                                        xqf[:, k * GRP + j * JW:k * GRP + j * JW + JW],
                                        start=first, stop=last)
                    items.append((em, [wgs, xqf]))
                mm_group(p, pb, items)
                p.op("s", lambda s, m=m, pb=pb: s.activation(
                    hg1[:, 0:JW], pb[:, 0:JW], AF.Silu, scale=scg[:, m:m + 1]),
                    reads=[pb, scg], writes=[hg1])
                pb2 = pbC if j % 2 == 0 else pbD
                items = []
                for k in range(MT):
                    def em(t, m=m, j=j, k=k, pb2=pb2, first=(k == 0), last=(k == MT - 1)):
                        return t.matmul(pb2[:, 0:JW],
                                        wus[:, k * 128:(k + 1) * 128],
                                        xqf[:, k * GRP + j * JW:k * GRP + j * JW + JW],
                                        start=first, stop=last)
                    items.append((em, [wus, xqf]))
                mm_group(p, pb2, items)
                p.op("s", lambda s, m=m, pb2=pb2: s.activation(
                    hg2[:, 0:JW], pb2[:, 0:JW], AF.Identity, scale=scu[:, m:m + 1]),
                    reads=[pb2, scu], writes=[hg2])
                p.op("v", lambda v: v.tensor_tensor(
                    hg1[:, 0:JW], hg1[:, 0:JW], hg2[:, 0:JW], ALU.mult),
                    reads=[hg1, hg2], writes=[hg1])
                p.op("v", lambda v: v.tensor_reduce(
                    gmaxt[:, 1:2], hg1[:, 0:JW], AXX, ALU.max,
                    apply_absolute_value=True), reads=[hg1], writes=[gmaxt])
                p.op("v", lambda v: v.tensor_tensor(
                    gmaxt[:, 2:3], gmaxt[:, 2:3], gmaxt[:, 1:2], ALU.max),
                    reads=[gmaxt], writes=[gmaxt])
                p.op("q", lambda q, g_=g_, m=m, j=j: q.dma_start(
                    out=scr_hpre.h[m, :, g_ * GRP + j * JW:g_ * GRP + j * JW + JW],
                    in_=hg1[:, 0:JW]), reads=[hg1], writes=[scr_hpre], dma=True)

    global_max_and_bcast(2, scr_mh, scr_mho, 1)
    p.op("v", lambda v: v.scalar_tensor_tensor(
        qsc[:, 1:2], qsc[:, 1:2], 1.0 / QMAX, qsc[:, 3:4], ALU.mult, ALU.mult),
        reads=[qsc], writes=[qsc])
    p.op("v", lambda v: v.reciprocal(qsc[:, 5:6], qsc[:, 1:2]),
         reads=[qsc], writes=[qsc])
    p.op("v", lambda v: v.tensor_scalar(
        scdt[:, :], scdt[:, :], qsc[:, 1:2], None, ALU.mult),
        reads=[scdt, qsc], writes=[scdt])

    hqf = sb3("hqf", [128, KQ * 512], BF16)
    wds = sb3("wds", [128, KQ * 128], BF16)
    for g_ in range(NGRP):
        for j in range(JPG):
            col0 = g_ * GRP + j * JW
            for m in range(KQ):
                quant_chain(
                    JW,
                    lambda q, m=m, col0=col0: q.dma_start(
                        out=qv1[:, 0:JW],
                        in_=bass.AP(scr_hpre.h.tensor,
                                    scr_hpre.h.offset + m * 128 * MYT + col0,
                                    [[MYT, 128], [1, JW]])),
                    [scr_hpre], hqf, (m * 512, m * 512 + JW), 5,
                    src_stream="q", src_dma=True)
            for m in range(MT):
                p.op("q", lambda q, m=m: q.dma_start(
                    out=wds[:, :], in_=scr_wdT.h[m]), reads=[scr_wdT], writes=[wds],
                    dma=True)
                pb = pbA if m % 2 == 0 else pbB
                items = []
                for k in range(KQ):
                    def em(t, m=m, k=k, pb=pb, first=(k == 0), last=(k == KQ - 1)):
                        return t.matmul(pb[:, 0:JW],
                                        wds[:, k * 128:(k + 1) * 128],
                                        hqf[:, k * 512:k * 512 + JW],
                                        start=first, stop=last)
                    items.append((em, [wds, hqf]))
                mm_group(p, pb, items)
                p.op("s", lambda s, m=m, pb=pb: s.activation(
                    ofe3[:, m * JW:m * JW + JW], pb[:, 0:JW], AF.Identity,
                    scale=scdt[:, m:m + 1]), reads=[pb, scdt], writes=[ofe3])
            for i in range(JI):
                for m in range(MT):
                    p.op("t", lambda t, m=m, i=i: t.transpose(
                        ptF[:, 0:128],
                        ofe3[:, m * JW + i * 128:m * JW + i * 128 + 128],
                        idf[:, :]), reads=[ofe3, idf], writes=[ptF])
                    p.op("s", lambda s, m=m, i=i: s.copy(
                        otk3[:, i * DM + m * 128:i * DM + m * 128 + 128], ptF[:, 0:128]),
                        reads=[ptF], writes=[otk3])
            for i in range(JI):
                row0 = col0 + i * 128
                p.op("q", lambda q, row0=row0: q.dma_start(
                    out=u1t3[:, 0:DM], in_=scr_xp.h[row0:row0 + 128, :]),
                    reads=[scr_xp], writes=[u1t3], dma=True)
                p.op("v", lambda v, i=i: v.tensor_tensor(
                    otk3[:, i * DM:(i + 1) * DM], otk3[:, i * DM:(i + 1) * DM],
                    u1t3[:, 0:DM], ALU.add), reads=[otk3, u1t3], writes=[otk3])
                p.op("q", lambda q, row0=row0, i=i: q.dma_start(
                    out=out.h[row0:row0 + 128, :], in_=otk3[:, i * DM:(i + 1) * DM]),
                    reads=[otk3], writes=[out], dma=True)

    p.wait("q", out.last_w)
    p.emit()
    es_p3.close()
    es.close()
    return nc


_NC_CACHE = {}


def kernel(L0_attn_x, L0_attn_in_proj_w, L0_attn_conv_w, L0_attn_conv_b,
           L0_attn_x_proj_w, L0_attn_dt_proj_w, L0_attn_dt_proj_b, L0_attn_A_log,
           L0_attn_D, L0_attn_out_proj_w, L0_attn_ln1_g, L0_attn_ln1_b,
           L0_attn_ln2_g, L0_attn_ln2_b, L0_attn_ffn_w1, L0_attn_ffn_b1,
           L0_attn_ffn_w2, L0_attn_ffn_b2, L0_ffn_clip_x, L0_ffn_clip_down,
           L0_ffn_w_gate, L0_ffn_w_up, L0_ffn_w_down):
    x = np.ascontiguousarray(np.asarray(L0_attn_x, dtype=np.float32))
    B, T, _ = x.shape
    TKN = T
    MYT = T // 2
    TC = 512
    NCH = TKN // TC

    f32 = lambda a: np.ascontiguousarray(np.asarray(a, dtype=np.float32))
    w_in = f32(L0_attn_in_proj_w)
    w_cv = f32(L0_attn_conv_w)[:, 0, :]
    b_cv = f32(L0_attn_conv_b)
    w_xp = f32(L0_attn_x_proj_w)
    w_dt = f32(L0_attn_dt_proj_w)
    b_dt = f32(L0_attn_dt_proj_b)
    a_lg = f32(L0_attn_A_log)
    d_v = f32(L0_attn_D)
    w_op = f32(L0_attn_out_proj_w)

    key = (TKN, TC)
    if key not in _NC_CACHE:
        _NC_CACHE[key] = build(TKN=TKN, TC=TC)
    nc = _NC_CACHE[key]

    in_maps = []
    for core in range(8):
        b = core // 2
        h = core % 2
        sl = slice(h * 1024, (h + 1) * 1024)
        im = {
            "xfull": x[b],
            "xmine": np.ascontiguousarray(
                x[b].reshape(NCH, 2, TC // 2, DM)[:, h].reshape(MYT, DM)),
            "w_inproj": np.ascontiguousarray(
                np.concatenate([w_in[sl], w_in[2048 + h * 1024: 2048 + (h + 1) * 1024]], 0)),
            "w_conv": np.ascontiguousarray(w_cv[sl]),
            "b_conv": np.ascontiguousarray(b_cv[sl]),
            "w_xproj": np.ascontiguousarray(w_xp[:, sl]),
            "w_dtproj": np.ascontiguousarray(w_dt[sl]),
            "b_dtproj": np.ascontiguousarray(b_dt[sl]),
            "alog": np.ascontiguousarray(a_lg[sl]),
            "dvec": np.ascontiguousarray(d_v[sl]),
            "w_outproj": np.ascontiguousarray(w_op[:, sl]),
            "ln1_g": f32(L0_attn_ln1_g), "ln1_b": f32(L0_attn_ln1_b),
            "ln2_g": f32(L0_attn_ln2_g), "ln2_b": f32(L0_attn_ln2_b),
            "w_ffn1": f32(L0_attn_ffn_w1), "b_ffn1": f32(L0_attn_ffn_b1),
            "w_ffn2": f32(L0_attn_ffn_w2), "b_ffn2": f32(L0_attn_ffn_b2),
            "clip_x": f32(L0_ffn_clip_x), "clip_down": f32(L0_ffn_clip_down),
            "w_gate": f32(L0_ffn_w_gate), "w_up": f32(L0_ffn_w_up),
            "w_down": f32(L0_ffn_w_down),
        }
        in_maps.append(im)

    res = run_bass_kernel_spmd(nc, in_maps, core_ids=list(range(8)))

    y = np.empty((B, T, DM), dtype=np.float32)
    for core in range(8):
        b = core // 2
        h = core % 2
        o = np.asarray(res.results[core]["out"]).reshape(NCH, TC // 2, DM)
        yv = y[b].reshape(NCH, 2, TC // 2, DM)
        yv[:, h] = o
    return y
